# version 31
# baseline (speedup 1.0000x reference)
import numpy as np
from contextlib import ExitStack

import concourse.bass as bass
import concourse.mybir as mybir
from concourse.bass_utils import run_bass_kernel_spmd

F32 = mybir.dt.float32
BF16 = mybir.dt.bfloat16
AF = mybir.ActivationFunctionType
ALU = mybir.AluOpType
AX = mybir.AxisListType

NCORES = 8
S = 2048
NT = 16
C = 1024
CC = 8
H = 16
DH = 64
E = 8
HD = 512
EXP_C = 30.0


class Sem:
    __slots__ = ("h", "name", "n")

    def __init__(self, h, name):
        self.h = h
        self.name = name
        self.n = 0

    def mark(self):
        return (self, self.n)


class Plan:
    ENGS = ("sync", "gpsimd", "vector", "scalar", "tensor")

    def __init__(self):
        self.steps = []

    def step(self, eng, fn, waits=(), inc=None, inc_n=1):
        w = [x for x in waits if x is not None and x[1] > 0]
        if inc is not None:
            inc.n += inc_n
        self.steps.append((eng, fn, w, (inc, inc_n) if inc is not None else None))

    def replay(self, blk):
        floors = {e: {} for e in self.ENGS}

        def run(engine_name, eng):
            fl = floors[engine_name]
            for e, fn, waits, inc in self.steps:
                if e != engine_name:
                    continue
                for (sem, val) in waits:
                    if fl.get(sem.name, 0) >= val:
                        continue
                    eng.wait_ge(sem.h, val)
                    fl[sem.name] = val
                ins = fn(eng)
                if inc is not None:
                    assert ins is not None
                    ins.then_inc(inc[0].h, inc[1])

        @blk.sync
        def _(eng):
            run("sync", eng)

        @blk.gpsimd
        def _(eng):
            run("gpsimd", eng)

        @blk.vector
        def _(eng):
            run("vector", eng)

        @blk.scalar
        def _(eng):
            run("scalar", eng)

        @blk.tensor
        def _(eng):
            run("tensor", eng)


class Rot:
    def __init__(self, n):
        self.n = n
        self.rel = [None] * n

    def acq(self, i):
        s = i % self.n
        return s, ([self.rel[s]] if self.rel[s] is not None else [])

    def release(self, i, sem):
        self.rel[i % self.n] = sem.mark()


def build_program(n_sub=8):
    nc = bass.Bass("TRN2", target_bir_lowering=False)
    P = Plan()

    def din(name, shape):
        return nc.dram_tensor(name, shape, F32, kind="ExternalInput").ap()

    x_d = din("x", [2, S, C])
    wqkv_d = din("wqkv", [C, 3 * C])
    bqkv_d = din("bqkv", [3 * C])
    wproj_d = din("wproj", [C, C])
    bproj_d = din("bproj", [C])
    qn_d = din("qn", [DH])
    kn_d = din("kn", [DH])
    wqT_d = din("wqT", [C, C])
    wkT_d = din("wkT", [C, C])
    wvT_d = din("wvT", [C, C])
    woT_d = din("woT", [C, C])
    gw_d = din("gw", [C, E])
    w1_d = din("w1", [E, C, HD])
    b1f_d = din("b1f", [E * HD])
    w2_d = din("w2", [E, HD, C])
    b2_d = din("b2", [E, C])
    ws1_d = din("ws1", [C, HD])
    bs1_d = din("bs1", [HD])
    ws2_d = din("ws2", [HD, C])
    bs2_d = din("bs2", [C])
    rs_d = din("rs", [1])
    rcos_d = din("rcos", [S, 32])
    rsin_d = din("rsin", [S, DH])

    y_d = nc.dram_tensor("y", [2, S, C], F32, kind="ExternalOutput").ap()
    xa_d = nc.dram_tensor("xsa", [S, C], F32, kind="Internal").ap()
    xb_d = nc.dram_tensor("xsb", [S, C], F32, kind="Internal").ap()

    stack = ExitStack()
    eb = stack.enter_context

    def sb(name, shape, dt=F32):
        return eb(nc.sbuf_tensor(name, shape, dt))

    big = sb("big", [128, 8 * S], BF16)
    QF = sb("QF", [128, 8 * S], BF16)
    KF = sb("KF", [128, 8 * S], BF16)
    VB = sb("VB", [128, NT * H * 65], BF16)
    wsl = sb("wsl", [128, 3 * 4096], BF16)
    xstage = sb("xstage", [128, 2 * C], F32)
    xostage = sb("xostage", [128, 2 * C], F32)
    pool = sb("pool", [128, 2 * C], F32)
    PT = sb("PT", [128, 2 * 2048], BF16)
    ghT = sb("ghT", [128, 3 * 512], BF16)
    cos_sb = sb("cos_sb", [128, NT * 32], BF16)
    sin_sb = sb("sin_sb", [128, NT * DH], BF16)
    cmaskm = sb("cmaskm", [128, 1280], BF16)
    onesq_bf = sb("onesq_bf", [128, 512], BF16)
    eye = sb("eye", [128, 128], F32)
    brow = sb("brow", [1, 4 * 512], BF16)
    b2_sb = sb("b2_sb", [8, C], BF16)
    gw_sb = sb("gw_sb", [128, CC * E], BF16)
    ones_bf = sb("ones_bf", [1, 512], BF16)
    ones_f32 = sb("ones_f32", [1, 128], F32)
    stats = sb("stats", [128, 32], F32)
    nrms = [sb(f"nrm{i}", [128, 16], F32) for i in range(4)]
    nrmq = sb("nrmq", [128, 16], F32)
    nrmk = sb("nrmk", [128, 16], F32)
    gate_all = sb("gate_all", [128, NT * E], F32)
    gwork = sb("gwork", [128, 64], F32)
    qn_c = sb("qn_c", [64, 1], F32)
    kn_c = sb("kn_c", [64, 1], F32)
    qn_r = sb("qn_r", [1, DH], F32)
    kn_r = sb("kn_r", [1, DH], F32)
    rs_sb = sb("rs_sb", [1, 1], F32)
    rs_bc = sb("rs_bc", [128, 1], F32)
    biasA = sb("biasA", [128, 1], F32)
    biasB = sb("biasB", [128, 1], F32)
    smallw = sb("smallw", [1, 16], F32)
    eps5_c = sb("eps5_c", [128, 1], F32)
    eps6_c = sb("eps6_c", [128, 1], F32)
    eps30_c = sb("eps30_c", [128, 1], F32)

    onesq = xstage[:, 0:128]
    sums_sb = pool[0:1, 0:S]
    recip_sb = xostage[0:1, 0:S]
    Rsb = xostage[0:64, 0:S]
    gateT_all = PT[0:8, 0:S]

    pS = eb(nc.psum_tensor("pS", [128, 2048], F32))
    pO = eb(nc.psum_tensor("pO", [128, 2048], F32))

    LD = Sem(eb(nc.semaphore("LD")), "LD")
    STb = Sem(eb(nc.semaphore("STb")), "STb")
    WL = Sem(eb(nc.semaphore("WL")), "WL")
    LDs = [Sem(eb(nc.semaphore(f"LDs{i}")), f"LDs{i}") for i in range(2)]
    STs = [Sem(eb(nc.semaphore(f"STs{i}")), f"STs{i}") for i in range(2)]
    WLs = [Sem(eb(nc.semaphore(f"WLs{i}")), f"WLs{i}") for i in range(3)]
    WBs = [Sem(eb(nc.semaphore(f"WBs{i}")), f"WBs{i}") for i in range(4)]
    TE = Sem(eb(nc.semaphore("TE")), "TE")
    VE = Sem(eb(nc.semaphore("VE")), "VE")
    SE = Sem(eb(nc.semaphore("SE")), "SE")
    GE = Sem(eb(nc.semaphore("GE")), "GE")

    def barrier():
        marks = ([TE.mark(), VE.mark(), SE.mark(), LD.mark(), GE.mark(),
                  WL.mark()] +
                 [s.mark() for s in LDs + STs + WLs + WBs])
        for e in ("sync", "vector", "scalar", "tensor"):
            P.step(e, lambda eng: eng.nop(), waits=marks)

    def wdma(dst_ap, src_ap, waits=()):
        P.step("gpsimd",
               lambda eng, d=dst_ap, s=src_ap: eng.dma_start(out=d, in_=s),
               waits=waits, inc=WL, inc_n=16)
        return WL.n

    wslot_rot = Rot(3)
    wslot_i = [0]

    def wslot_load(src_ap, extra=None):
        i = wslot_i[0]
        wslot_i[0] += 1
        s, w = wslot_rot.acq(i)
        dst = wsl[:, s * 4096: s * 4096 + 4096]
        a = src_ap.shape[0] // 128
        sem = WLs[s]
        sem.n += 16
        P.step("gpsimd",
               lambda eng, d=dst.rearrange("p (a n) -> p a n", a=a),
               sp=src_ap.rearrange("(a p) n -> p a n", p=128):
               eng.dma_start(out=d, in_=sp),
               waits=w + ([extra] if extra is not None else []),
               inc=None)
        P.steps[-1] = (P.steps[-1][0], P.steps[-1][1], P.steps[-1][2],
                       (sem, 16))
        return dst, (sem, sem.n), (wslot_rot, i)

    brow_rot = Rot(4)
    brow_i = [0]

    def brow_load(src_ap):
        i = brow_i[0]
        brow_i[0] += 1
        s, w = brow_rot.acq(i)
        dst = brow[:, s * 512:(s + 1) * 512]
        sem = WBs[s]
        sem.n += 16
        P.step("gpsimd", lambda eng, d=dst[:, 0:src_ap.shape[-1]],
               sp=src_ap: eng.dma_start(out=d, in_=sp), waits=w)
        P.steps[-1] = (P.steps[-1][0], P.steps[-1][1], P.steps[-1][2],
                       (sem, 16))
        return dst[:, 0:src_ap.shape[-1]], (sem, sem.n), (brow_rot, i)

    def fm(cc, t0, n):
        return big[:, cc * S + t0: cc * S + t0 + n]

    wl_tabs = [0]

    def emit_setup():
        g = P.step
        g("gpsimd", lambda e: e.memset(ones_bf[:], 1.0))
        g("gpsimd", lambda e: e.memset(ones_f32[:], 1.0))
        g("gpsimd", lambda e: e.memset(VB[:], 1.0))
        g("gpsimd", lambda e: e.memset(onesq[:], 1.0), inc=GE)
        g("gpsimd", lambda e: e.affine_select(eye[:], onesq[:], [[-1, 128]],
                                              ALU.is_equal, 0.0, base=0,
                                              channel_multiplier=1),
          waits=[GE.mark()])
        g("gpsimd", lambda e: e.memset(onesq_bf[:], 1.0), inc=GE)
        g("gpsimd", lambda e: e.memset(eps5_c[:], 1e-5))
        g("gpsimd", lambda e: e.memset(eps30_c[:], 1e-30))
        g("gpsimd", lambda e: e.memset(eps6_c[:], 1e-6))
        gm = GE.mark()
        for (y0, w) in ((0, 512), (512, 512), (1024, 256)):
            g("gpsimd", lambda e, y0=y0, w=w: e.affine_select(
                cmaskm[:, y0:y0 + w], onesq_bf[:, 0:w], [[1, w]],
                ALU.is_ge, 0.0, base=y0 - 384, channel_multiplier=-1),
              waits=[gm])
        g("gpsimd", lambda e: e.nop(), inc=GE)
        ge_setup = GE.mark()
        wdma(b2_sb[:], b2_d)
        wdma(gw_sb[:].rearrange("p (a e) -> p a e", e=E),
             gw_d.rearrange("(a p) e -> p a e", p=128))
        wdma(cos_sb[:].rearrange("p (a d) -> p a d", d=32),
             rcos_d.rearrange("(a p) d -> p a d", p=128))
        wdma(sin_sb[:].rearrange("p (a d) -> p a d", d=DH),
             rsin_d.rearrange("(a p) d -> p a d", p=128))
        wl_tabs[0] = WL.n
        for dst, src in ((qn_c[:], qn_d[:, None]), (kn_c[:], kn_d[:, None]),
                         (qn_r[:], qn_d[None, :]), (kn_r[:], kn_d[None, :]),
                         (rs_sb[:], rs_d[None, :])):
            P.step("sync", lambda e, d=dst, s=src: e.dma_start(out=d, in_=s),
                   inc=LD, inc_n=16)
        ld_c = LD.n
        P.step("tensor", lambda e: e.matmul(pS[:, 0:1], ones_f32[:], rs_sb[:],
                                            start=True, stop=True),
               waits=[(LD, ld_c), ge_setup], inc=TE)
        t_rs = TE.n
        P.step("vector", lambda e: e.tensor_copy(rs_bc[:], pS[:, 0:1]),
               waits=[(TE, t_rs)], inc=VE)
        P.step("vector", lambda e: e.reduce_max(smallw[:, 0:1], qn_r[:], AX.X,
                                                apply_absolute_value=True),
               waits=[(LD, ld_c)])
        P.step("vector", lambda e: e.reduce_max(smallw[:, 1:2], kn_r[:], AX.X,
                                                apply_absolute_value=True),
               inc=VE)
        P.step("vector", lambda e: e.tensor_tensor(smallw[:, 2:3], smallw[:, 0:1],
                                                   smallw[:, 1:2], ALU.mult),
               waits=[VE.mark()], inc=VE)
        P.step("vector", lambda e: e.tensor_scalar(smallw[:, 3:4], smallw[:, 2:3],
                                                   -8.0, EXP_C, ALU.mult,
                                                   ALU.add), waits=[VE.mark()],
               inc=VE)
        v_ba = VE.n
        P.step("tensor", lambda e: e.matmul(pS[:, 4:5], ones_f32[:],
                                            smallw[:, 3:4], start=True,
                                            stop=True),
               waits=[(VE, v_ba), ge_setup], inc=TE)
        t_ba = TE.n
        P.step("vector", lambda e: e.tensor_copy(biasA[:], pS[:, 4:5]),
               waits=[(TE, t_ba)], inc=VE)

    def emit_ln(x_src):
        trash = xostage[:, 1024:2048]
        xnscr = PT[:].bitcast(F32)[:, 1024:2048]
        xrot, xnrot, trot = Rot(2), Rot(1), Rot(4)
        ln_marks = [None] * NT
        bank_marks = [None] * 4
        for u in range(NT // 2):
            st = stats[:, (u % 2) * 16:(u % 2) * 16 + 16]
            ld_ms = []
            se_sq = None
            ve_sum = None
            for tt in range(2):
                i = 2 * u + tt
                sl, w = xrot.acq(i)
                P.step("sync", lambda e, sl=sl, i=i: e.dma_start(
                    out=xstage[:, sl * C:(sl + 1) * C],
                    in_=x_src[i * 128:(i + 1) * 128, :]), waits=w,
                    inc=LDs[sl], inc_n=16)
                ld_m = LDs[sl].mark()
                ld_ms.append(ld_m)
                xs = xstage[:, sl * C:(sl + 1) * C]
                P.step("vector", lambda e, xs=xs, st=st, tt=tt: e.reduce_sum(
                    st[:, tt:tt + 1], xs, AX.X), waits=[ld_m],
                    inc=(VE if tt == 1 else None))
                P.step("scalar", lambda e, xs=xs, st=st, tt=tt: e.activation(
                    trash, xs, AF.Square, accum_out=st[:, 2 + tt:3 + tt]),
                    waits=[ld_m], inc=(SE if tt == 1 else None))
            ve_sum = VE.mark()
            se_sq = SE.mark()
            P.step("vector", lambda e, st=st: e.tensor_scalar_mul(
                st[:, 4:6], st[:, 0:2], 1.0 / C), waits=[ve_sum], inc=VE)
            P.step("vector", lambda e, st=st: e.tensor_tensor(
                st[:, 6:8], st[:, 4:6], st[:, 4:6], ALU.mult),
                waits=[VE.mark()], inc=VE)
            P.step("vector", lambda e, st=st: e.scalar_tensor_tensor(
                st[:, 8:10], st[:, 2:4], 1.0 / C, st[:, 6:8],
                ALU.mult, ALU.subtract), waits=[se_sq, VE.mark()], inc=VE)
            ve_var = VE.mark()
            P.step("scalar", lambda e, st=st: e.activation(
                st[:, 10:12], st[:, 8:10], AF.Sqrt, bias=eps5_c[:, 0:1]),
                waits=[ve_var], inc=SE)
            se_srt = SE.mark()
            P.step("vector", lambda e, st=st: e.reciprocal(
                st[:, 12:14], st[:, 10:12]), waits=[se_srt], inc=VE)
            P.step("vector", lambda e, st=st: e.tensor_tensor(
                st[:, 14:16], st[:, 4:6], st[:, 12:14], ALU.mult),
                waits=[VE.mark()], inc=VE)
            P.step("vector", lambda e, st=st: e.tensor_scalar_mul(
                st[:, 14:16], st[:, 14:16], -1.0), waits=[VE.mark()], inc=VE)
            ve_nm = VE.mark()
            for tt in range(2):
                i = 2 * u + tt
                sl = i % 2
                xs = xstage[:, sl * C:(sl + 1) * C]
                xsl, xw = xnrot.acq(i)
                xn = xnscr
                P.step("scalar", lambda e, xs=xs, st=st, xn=xn, tt=tt:
                       e.activation(xn, xs, AF.Identity,
                                    bias=st[:, 14 + tt:15 + tt],
                                    scale=st[:, 12 + tt:13 + tt]),
                       waits=[ve_nm] + xw, inc=SE)
                se_xn = SE.mark()
                xrot.release(i, SE)
                for half in range(2):
                    bk = (i * 2 + half) % 4
                    ts, tw = trot.acq(i * 2 + half)
                    for j4 in range(4):
                        j = half * 4 + j4
                        ps = pS[:, bk * 512 + j4 * 128:
                                bk * 512 + j4 * 128 + 128]
                        P.step("tensor", lambda e, xn=xn, j=j, ps=ps:
                               e.transpose(ps, xn[:, j * 128:(j + 1) * 128],
                                           eye[:]),
                               waits=([se_xn] + tw) if j4 == 0 else [],
                               inc=(TE if j4 == 3 else None))
                    te_t = TE.mark()
                    dst = bass.AP(big[:].tensor, half * 4 * S + i * 128,
                                  [[8 * S, 128], [S, 4], [1, 128]])
                    srcv = pS[:, bk * 512: bk * 512 + 512].rearrange(
                        "p (a n) -> p a n", a=4)
                    P.step("vector", lambda e, d=dst, s=srcv: e.tensor_copy(
                        d, s), waits=[te_t], inc=VE)
                    trot.release(i * 2 + half, VE)
                    bank_marks[bk] = VE.mark()
                ln_marks[i] = VE.mark()
                xnrot.release(i, TE)
        return ln_marks, bank_marks

    def emit_qkv(is_b, wblocks, ln_marks=None, bank_marks=None):
        ropescr = pool[:, 0:1024]
        qkvscr = pool[:, 1024:2048] if is_b else pool
        sqscr = xostage[:, 0:1024]
        nscr = 2 if is_b else 4
        mmrot, scrot, sqrot, trot = Rot(4), Rot(nscr), Rot(2), Rot(4)
        if bank_marks is not None:
            mmrot.rel = list(bank_marks)
        rope_prev = [None]
        it = [0]
        qpend = []
        for bi, (kind, hb, wsrc, bsrc) in enumerate(wblocks):
            wv, wl, (wr, wi) = wslot_load(wsrc)
            bv = None
            bwl = None
            if bsrc is not None:
                bv, bwl, (br, bi2) = brow_load(bsrc)

            def stage1(t, k, wv=wv, bv=bv, wl=wl, bwl=bwl):
                ms, mw = mmrot.acq(k)
                pslot = pS[:, ms * 512:(ms + 1) * 512]
                ww = [wl] + ([bwl] if bwl is not None else [])
                if ln_marks is not None:
                    ww.append(ln_marks[t])

                def mk_mm(e, pslot=pslot, wv=wv, t=t, bv=bv):
                    ins = None
                    for cc2 in range(CC):
                        ins = e.matmul(pslot, fm(cc2, t * 128, 128),
                                       wv[:, cc2 * 512:(cc2 + 1) * 512],
                                       start=(cc2 == 0),
                                       stop=(cc2 == CC - 1 and bv is None))
                    if bv is not None:
                        ins = e.matmul(pslot, ones_bf[:, 0:128], bv,
                                       start=False, stop=True)
                    return ins
                P.step("tensor", mk_mm, waits=ww + mw, inc=TE)
                return pslot, TE.mark()

            def stage2(t, k, pslot, te_mm, kind=kind, hb=hb):
                if kind == "v":
                    dst = bass.AP(VB[:].tensor, t * H * 65 + hb * 65,
                                  [[NT * H * 65, 128], [65, 8], [1, 64]])
                    P.step("vector", lambda e, d=dst, ps=pslot: e.tensor_copy(
                        d, ps.rearrange("p (h d) -> p h d", d=64)),
                        waits=[te_mm], inc=VE)
                    mmrot.release(k, VE)
                    return
                ss, sw = scrot.acq(k)
                scr = qkvscr[:, ss * 512:(ss + 1) * 512]
                nsl = k % 4
                P.step("scalar", lambda e, d=scr, ps=pslot: e.activation(
                    d, ps, AF.Copy), waits=[te_mm] + sw, inc=SE)
                mmrot.release(k, SE)
                se_cp = SE.mark()
                qs, qw = sqrot.acq(k)
                sq = sqscr[:, qs * 512:(qs + 1) * 512]
                P.step("scalar", lambda e, d=sq, s=scr: e.activation(
                    d, s, AF.Square), waits=qw + [se_cp], inc=SE)
                se_sq = SE.mark()
                n8 = nrms[nsl][:, 0:8]
                P.step("vector", lambda e, d=n8, s=sq: e.tensor_reduce(
                    d, s.rearrange("p (h d) -> p h d", d=64), AX.X, ALU.add),
                    waits=[se_sq], inc=VE)
                sqrot.release(k, VE)
                if not is_b:
                    r8 = nrms[nsl][:, 8:16]
                    ve_n8 = VE.mark()
                    P.step("scalar", lambda e, d=r8, s=n8: e.activation(
                        d, s, AF.Sqrt, scale=1.0 / DH, bias=eps6_c[:, 0:1]),
                        waits=[ve_n8], inc=SE)
                    se_r = SE.mark()
                    P.step("vector", lambda e, d=r8: e.reciprocal(d, d),
                           waits=[se_r], inc=VE)
                    ve_rc = VE.mark()
                    r8v = bass.AP(r8.tensor, r8.offset,
                                  [[16, 128], [1, 8], [0, 64]])
                    P.step("vector", lambda e, s=scr, rv=r8v:
                           e.tensor_tensor(
                               s.rearrange("p (h d) -> p h d", d=64),
                               s.rearrange("p (h d) -> p h d", d=64),
                               rv, ALU.mult),
                           waits=[ve_rc], inc=VE)
                    ve_ready = VE.mark()
                else:
                    nm_dst = (nrmq if kind == "q" else nrmk)[:, hb:hb + 8]
                    if t == 0:
                        P.step("vector", lambda e, d=nm_dst, s=n8:
                               e.tensor_copy(d, s))
                    else:
                        P.step("vector", lambda e, d=nm_dst, s=n8:
                               e.tensor_tensor(d, d, s, ALU.max))
                    cosv = bass.AP(cos_sb[:].tensor, t * 32,
                                   [[NT * 32, 128], [0, 8], [0, 2], [1, 32]])
                    sinv = bass.AP(sin_sb[:].tensor, t * DH,
                                   [[NT * DH, 128], [0, 8], [32, 2], [1, 32]])
                    swv = bass.AP(pool[:].tensor, 1024 + ss * 512 + 32,
                                  [[2048, 128], [64, 8], [-32, 2], [1, 32]])
                    tmp = ropescr[:, 0:512]
                    cq = ropescr[:, 512:1024]
                    P.step("gpsimd", lambda e, tmp=tmp, swv=swv, sinv=sinv:
                           e.tensor_tensor(
                               tmp.rearrange("p (h x d) -> p h x d", x=2, d=32),
                               swv, sinv, ALU.mult),
                           waits=[se_sq, (WL, wl_tabs[0])] +
                           ([rope_prev[0]] if rope_prev[0] else []), inc=GE)
                    P.step("gpsimd", lambda e, cq=cq, s=scr, cosv=cosv:
                           e.tensor_tensor(
                               cq.rearrange("p (h x d) -> p h x d", x=2, d=32),
                               s.rearrange("p (h x d) -> p h x d", x=2, d=32),
                               cosv, ALU.mult), waits=[GE.mark()], inc=GE)
                    P.step("vector", lambda e, s=scr, tmp=tmp, cq=cq:
                           e.tensor_tensor(s, tmp, cq, ALU.add),
                           waits=[se_sq, GE.mark()], inc=VE)
                    ve_ready = VE.mark()
                    rope_prev[0] = ve_ready
                dstbuf = QF if kind == "q" else KF
                scale_c = (qn_c if kind == "q" else kn_c) if not is_b else None
                for half in range(2):
                    bk, tw2 = trot.acq(k * 2 + half)
                    for j4 in range(4):
                        j = half * 4 + j4
                        ps = pO[:, bk * 512 + j4 * 128:
                                bk * 512 + j4 * 128 + 128]
                        P.step("tensor", lambda e, s=scr, j=j, ps=ps:
                               e.transpose(ps[0:64, :],
                                           s[:, j * 64:(j + 1) * 64], eye[:]),
                               waits=([ve_ready] + tw2) if j4 == 0 else [],
                               inc=(TE if j4 == 3 else None))
                    te_tr = TE.mark()
                    h0 = hb + half * 4
                    for par in range(2):
                        row = ((h0 + par) % 2) * 64
                        pt0 = (h0 + par) // 2
                        dst = bass.AP(dstbuf[:].tensor,
                                      row * 8 * S + pt0 * S + t * 128,
                                      [[8 * S, 64], [S, 2], [1, 128]])
                        srcv = bass.AP(pO[:].tensor,
                                       bk * 512 + par * 128,
                                       [[2048, 64], [256, 2], [1, 128]])
                        if scale_c is not None:
                            P.step("vector", lambda e, d=dst, s=srcv,
                                   sc=scale_c:
                                   e.tensor_scalar_mul(d, s, sc[:, 0:1]),
                                   waits=[te_tr], inc=VE)
                        else:
                            P.step("vector", lambda e, d=dst, s=srcv:
                                   e.tensor_copy(d, s),
                                   waits=[te_tr], inc=VE)
                    trot.release(k * 2 + half, VE)
                scrot.release(k, TE)

            def mkdone(wr=wr, wi=wi, has_b=(bsrc is not None)):
                br_, bi_ = (br, bi2) if bsrc is not None else (None, None)

                def done():
                    wr.release(wi, TE)
                    if has_b:
                        br_.release(bi_, TE)
                return done

            for t in range(NT):
                k = it[0]
                it[0] += 1
                s1 = stage1(t, k)
                qpend.append((
                    (lambda t=t, k=k, ps=s1[0], te=s1[1], f=stage2:
                     f(t, k, ps, te)),
                    mkdone() if t == NT - 1 else None))
                if len(qpend) > 2:
                    f2, dn = qpend.pop(0)
                    f2()
                    if dn is not None:
                        dn()

        while qpend:
            f2, dn = qpend.pop(0)
            f2()
            if dn is not None:
                dn()

    def emit_biasB():
        P.step("vector", lambda e: e.reduce_max(nrms[0][:, 0:1], nrmq[:], AX.X))
        P.step("vector", lambda e: e.reduce_max(nrms[0][:, 1:2], nrmk[:], AX.X),
               inc=VE)
        v0 = VE.mark()
        P.step("tensor", lambda e: e.transpose(pS[0:1, 0:128], nrms[0][:, 0:1],
                                               eye[:]), waits=[v0])
        P.step("tensor", lambda e: e.transpose(pS[0:1, 128:256], nrms[0][:, 1:2],
                                               eye[:]), inc=TE)
        t0 = TE.mark()
        P.step("vector", lambda e: e.reduce_max(smallw[:, 4:5],
                                                pS[0:1, 0:128], AX.X),
               waits=[t0])
        P.step("vector", lambda e: e.reduce_max(smallw[:, 5:6],
                                                pS[0:1, 128:256], AX.X),
               inc=VE)
        P.step("vector", lambda e: e.tensor_tensor(
            smallw[:, 6:7], smallw[:, 4:5], smallw[:, 5:6], ALU.mult),
            waits=[VE.mark()], inc=VE)
        v1 = VE.mark()
        P.step("scalar", lambda e: e.activation(smallw[:, 7:8], smallw[:, 6:7],
                                                AF.Sqrt), waits=[v1], inc=SE)
        s0 = SE.mark()
        P.step("vector", lambda e: e.tensor_scalar(
            smallw[:, 8:9], smallw[:, 7:8], -0.125, EXP_C, ALU.mult, ALU.add),
            waits=[s0], inc=VE)
        v2 = VE.mark()
        P.step("tensor", lambda e: e.matmul(pS[:, 8:9], ones_f32[:],
                                            smallw[:, 8:9], start=True,
                                            stop=True), waits=[v2], inc=TE)
        t1 = TE.mark()
        P.step("vector", lambda e: e.tensor_copy(biasB[:], pS[:, 8:9]),
               waits=[t1], inc=VE)

    attn_state = {"s_free": [None, None],
                  "pt_free": [None, None, None, None],
                  "pO_free": [None, None], "par": 0}

    def emit_attn(causal, bias_bc):
        st_ = attn_state
        tail_pending = [None]

        def emit_sweep(pair, qc):
            par = st_["par"]
            st_["par"] ^= 1
            kt_hi = min(NT - 1, qc * 4 + 3) if causal else NT - 1

            def stage1(kt):
                sl = kt % 2
                waits = ([st_["s_free"][sl]]
                         if st_["s_free"][sl] is not None else [])

                def mk_s(e, kt=kt, pair=pair, qc=qc, sl=sl):
                    ins = None
                    for side in range(2):
                        r0 = side * 64
                        ins = e.matmul(
                            pS[:, sl * 1024 + side * 512:
                               sl * 1024 + side * 512 + 512],
                            KF[r0:r0 + 64, pair * S + kt * 128:
                               pair * S + kt * 128 + 128],
                            QF[r0:r0 + 64, pair * S + qc * 512:
                               pair * S + qc * 512 + 512],
                            start=True, stop=True, tile_position=(r0, 0))
                    return ins
                P.step("tensor", mk_s, waits=waits, inc=TE)
                te_s = TE.mark()
                pt4 = kt % 4
                ptw = ([st_["pt_free"][pt4]]
                       if st_["pt_free"][pt4] is not None else [])
                P.step("scalar", lambda e, sl=sl, pt4=pt4:
                       e.activation(PT[:, pt4 * 1024: pt4 * 1024 + 1024],
                                    pS[:, sl * 1024: sl * 1024 + 1024],
                                    AF.Exp, bias=bias_bc[:, 0:1], scale=0.125),
                       waits=[te_s] + ptw, inc=SE)
                st_["s_free"][sl] = SE.mark()
                se_exp = SE.mark()
                ve_mask = None
                if causal and kt // 4 == qc:
                    moff = 384 - (kt % 4) * 128
                    pv = bass.AP(PT[:].tensor, pt4 * 1024,
                                 [[2 * 2048, 128], [512, 2], [1, 512]])
                    mv = bass.AP(cmaskm[:].tensor, moff,
                                 [[1280, 128], [0, 2], [1, 512]])
                    P.step("vector", lambda e, pv=pv, mv=mv:
                           e.tensor_tensor(pv, pv, mv, ALU.mult),
                           waits=[se_exp], inc=VE)
                    ve_mask = VE.mark()
                return se_exp, ve_mask

            def stage2(kt, se_exp, ve_mask):
                pt4 = kt % 4
                pvw = [se_exp] + ([ve_mask] if ve_mask is not None else [])
                if kt == 0 and st_["pO_free"][par] is not None:
                    pvw.append(st_["pO_free"][par])

                def mk_pv(e, kt=kt, pt4=pt4, pair=pair, par=par):
                    ins = None
                    for side in range(2):
                        h = 2 * pair + side
                        ins = e.matmul(
                            pO[0:65, par * 1024 + side * 512:
                               par * 1024 + side * 512 + 512],
                            VB[:, kt * H * 65 + h * 65:
                               kt * H * 65 + h * 65 + 65],
                            PT[:, pt4 * 1024 + side * 512:
                               pt4 * 1024 + side * 512 + 512],
                            start=(kt == 0), stop=(kt == kt_hi))
                    return ins
                P.step("tensor", mk_pv, waits=pvw, inc=TE)
                st_["pt_free"][pt4] = TE.mark()

            pend = None
            first = True
            for kt in range(kt_hi + 1):
                s1 = stage1(kt)
                if first and tail_pending[0] is not None:
                    tail_pending[0]()
                    tail_pending[0] = None
                first = False
                if pend is not None:
                    stage2(*pend)
                pend = (kt, s1[0], s1[1])
            stage2(*pend)
            te_pv = TE.mark()

            def mk_tail(pair=pair, qc=qc, par=par, te_pv=te_pv):
                rrow = xostage[0:1, par * 1024: par * 1024 + 1024]
                lnrow = pool[0:1, 0:1024]
                P.step("scalar", lambda e, par=par: e.activation(
                    lnrow, pO[64:65, par * 1024: par * 1024 + 1024], AF.Ln,
                    bias=eps30_c[0:1, 0:1]),
                    waits=[te_pv], inc=SE)
                P.step("scalar", lambda e, rrow=rrow: e.activation(
                    rrow, lnrow, AF.Exp, scale=-1.0),
                    waits=[SE.mark()], inc=SE)
                ve_rec = SE.mark()

                def mk_r(e, par=par, rrow=rrow):
                    ins = None
                    for side in range(2):
                        ins = e.matmul(
                            pO[64:128, par * 1024 + side * 512:
                               par * 1024 + side * 512 + 512],
                            ones_f32[:, 0:64],
                            rrow[:, side * 512: side * 512 + 512],
                            start=True, stop=True)
                    return ins
                P.step("tensor", mk_r, waits=[ve_rec], inc=TE)
                te_r = TE.mark()
                rs64 = xostage[0:64, par * 1024: par * 1024 + 1024]
                P.step("vector", lambda e, d=rs64, par=par: e.tensor_copy(
                    d, pO[64:128, par * 1024: par * 1024 + 1024]),
                    waits=[te_r], inc=VE)
                vcpy = VE.mark()
                for side in range(2):
                    dst = big[side * 64: side * 64 + 64,
                              pair * S + qc * 512: pair * S + qc * 512 + 512]
                    P.step("vector", lambda e, d=dst, side=side, par=par,
                           rs64=rs64: e.tensor_tensor(
                               d,
                               pO[0:64, par * 1024 + side * 512:
                                  par * 1024 + side * 512 + 512],
                               rs64[:, side * 512: side * 512 + 512],
                               ALU.mult),
                           waits=[vcpy], inc=VE)
                st_["pO_free"][par] = VE.mark()
            tail_pending[0] = mk_tail

        for pair in range(8):
            for qc in range(4):
                emit_sweep(pair, qc)
        tail_pending[0]()
        tail_pending[0] = None

    def emit_proj(w_dram, b_dram, x_src, x_dst):
        wvs = [wslot_load(w_dram[:, db * 512:(db + 1) * 512])
               for db in range(2)]
        bvs = ([brow_load(b_dram[None, db * 512:(db + 1) * 512])
                for db in range(2)] if b_dram is not None else None)
        xrot, orot, prot = Rot(2), Rot(2), Rot(2)
        for t in range(NT):
            sl, w = xrot.acq(t)
            P.step("sync", lambda e, sl=sl, t=t: e.dma_start(
                out=xstage[:, sl * C:(sl + 1) * C],
                in_=x_src[t * 128:(t + 1) * 128, :]), waits=w,
                inc=LDs[sl], inc_n=16)
            ld_m = LDs[sl].mark()
            ps_i, pw = prot.acq(t)
            pslot = pS[:, ps_i * 1024:(ps_i + 1) * 1024]
            for db in range(2):
                wv, wl, _ = wvs[db]
                bv, bwl = (bvs[db][0], bvs[db][1]) if bvs is not None \
                    else (None, None)
                ww = [wl] + ([bwl] if bv is not None else [])

                def mk(e, wv=wv, db=db, t=t, pslot=pslot, bv=bv):
                    ins = None
                    for cc2 in range(CC):
                        ins = e.matmul(
                            pslot[:, db * 512: db * 512 + 512],
                            big[:, cc2 * S + t * 128: cc2 * S + t * 128 + 128],
                            wv[:, cc2 * 512:(cc2 + 1) * 512],
                            start=(cc2 == 0),
                            stop=(cc2 == CC - 1 and bv is None))
                    if bv is not None:
                        ins = e.matmul(pslot[:, db * 512: db * 512 + 512],
                                       ones_bf[:, 0:128], bv,
                                       start=False, stop=True)
                    return ins
                P.step("tensor", mk, waits=ww + (pw if db == 0 else []),
                       inc=TE)
            te_m = TE.mark()
            osl, ow = orot.acq(t)
            xo = xostage[:, osl * C:(osl + 1) * C]
            P.step("vector", lambda e, xo=xo, pslot=pslot, sl=sl:
                   e.tensor_tensor(xo, xstage[:, sl * C:(sl + 1) * C], pslot,
                                   ALU.add),
                   waits=[te_m, ld_m] + ow, inc=VE)
            prot.release(t, VE)
            xrot.release(t, VE)
            ve_m = VE.mark()
            P.step("sync", lambda e, xo=xo, t=t: e.dma_start(
                out=x_dst[t * 128:(t + 1) * 128, :], in_=xo),
                waits=[ve_m], inc=STs[osl], inc_n=16)
            orot.release(t, STs[osl])
        for db in range(2):
            _, _, (wr, wi) = wvs[db]
            wr.release(wi, TE)
            if bvs is not None:
                _, _, (br2, bi2) = bvs[db]
                br2.release(bi2, TE)

    def emit_gate(ln_marks=None, bank_marks=None):
        grot = Rot(2)
        gwrot = Rot(2)
        if bank_marks is not None:
            grot.rel = [bank_marks[0], bank_marks[1]]
        for t in range(NT):
            gs_i, gw2 = grot.acq(t)
            _, gww = gwrot.acq(t)
            pslot = pS[:, gs_i * 512: gs_i * 512 + 8]

            def mk(e, t=t, pslot=pslot):
                ins = None
                for cc2 in range(CC):
                    ins = e.matmul(pslot, fm(cc2, t * 128, 128),
                                   gw_sb[:, cc2 * E:(cc2 + 1) * E],
                                   start=(cc2 == 0), stop=(cc2 == CC - 1))
                return ins
            gww2 = list(gw2)
            if ln_marks is not None:
                gww2.append(ln_marks[t])
            P.step("tensor", mk, waits=gww2, inc=TE)
            te_g = TE.mark()
            gwk = gwork[:, (t % 2) * 32:(t % 2) * 32 + 32]
            gs8 = gwk[:, 0:8]
            P.step("scalar", lambda e, d=gs8, s=pslot: e.activation(
                d, s, AF.Sigmoid), waits=[te_g] + gww, inc=SE)
            se_g = SE.mark()
            grot.release(t, SE)
            P.step("vector", lambda e, d=gs8: e.tensor_scalar_mul(
                d, d, rs_bc[:, 0:1]), waits=[se_g], inc=VE)
            P.step("vector", lambda e, gwk=gwk, gs8=gs8: e.reduce_max(
                gwk[:, 8:9], gs8, AX.X), waits=[VE.mark()], inc=VE)
            P.step("vector", lambda e, gwk=gwk, gs8=gs8: e.tensor_scalar(
                gwk[:, 10:18], gs8, gwk[:, 8:9], -1e30,
                ALU.is_equal, ALU.mult), waits=[VE.mark()], inc=VE)
            P.step("vector", lambda e, gwk=gwk, gs8=gs8: e.tensor_tensor(
                gwk[:, 10:18], gs8, gwk[:, 10:18], ALU.add),
                waits=[VE.mark()], inc=VE)
            P.step("vector", lambda e, gwk=gwk: e.reduce_max(
                gwk[:, 9:10], gwk[:, 10:18], AX.X), waits=[VE.mark()], inc=VE)
            P.step("vector", lambda e, gwk=gwk: e.tensor_tensor(
                gwk[:, 8:9], gwk[:, 8:9], gwk[:, 9:10], ALU.add),
                waits=[VE.mark()], inc=VE)
            P.step("vector", lambda e, gwk=gwk: e.reciprocal(
                gwk[:, 8:9], gwk[:, 8:9]), waits=[VE.mark()], inc=VE)
            P.step("vector", lambda e, gwk=gwk, gs8=gs8:
                   e.tensor_single_scalar(gwk[:, 10:18], gs8, gwk[:, 9:10],
                                          ALU.is_ge), waits=[VE.mark()],
                   inc=VE)
            P.step("vector", lambda e, t=t, gwk=gwk, gs8=gs8:
                   e.scalar_tensor_tensor(
                       gate_all[:, t * E:(t + 1) * E], gs8, gwk[:, 8:9],
                       gwk[:, 10:18], ALU.mult, ALU.mult),
                   waits=[VE.mark()], inc=VE)
            ve_gate = VE.mark()
            gwrot.release(t, VE)
            P.step("tensor", lambda e, t=t: e.transpose(
                pO[0:8, 0:128], gate_all[:, t * E:(t + 1) * E], eye[:]),
                waits=[ve_gate], inc=TE)
            te_gt = TE.mark()
            P.step("vector", lambda e, t=t: e.tensor_copy(
                gateT_all[:, t * 128:(t + 1) * 128], pO[0:8, 0:128]),
                waits=[te_gt], inc=VE)

    def emit_ffn(x_src, x_dst):
        y_acc = QF[:].bitcast(F32)
        hfm = pool[:].bitcast(BF16)
        hrot = Rot(2)
        yrot = Rot(2)
        xrot, orot = Rot(2), Rot(2)
        ya_marks = {}
        pa = []
        pb = []
        h_last = [None]
        bank_free = [None] * 4

        def drain_all():
            while pa:
                f2a, dn = pa.pop(0)
                f2a()
                if dn is not None:
                    dn()
            while pb:
                f2b, dn = pb.pop(0)
                f2b()
                if dn is not None:
                    dn()

        git = [0]
        for tg in range(2):
            tiles = list(range(tg * 8, tg * 8 + 8))
            for ei in range(E + 1):
                shared = (ei == E)
                if shared:
                    w1v, wl1, (wr1, wi1) = wslot_load(ws1_d)
                    w2v, wl2, (wr2, wi2) = wslot_load(ws2_d,
                                                      extra=h_last[0])
                    bv, bwl, (br1, bj1) = brow_load(bs1_d[None, :])
                    b2v, b2wl, (br2, bj2) = brow_load(bs2_d[None, 0:512])
                    b3v, b3wl, (br3, bj3) = brow_load(bs2_d[None, 512:1024])
                else:
                    w1v, wl1, (wr1, wi1) = wslot_load(w1_d[ei])
                    w2v, wl2, (wr2, wi2) = wslot_load(w2_d[ei],
                                                      extra=h_last[0])
                    bv, bwl, (br1, bj1) = brow_load(
                        b1f_d[None, ei * HD: ei * HD + 512])
                    b2v = b3v = b2wl = b3wl = None

                def stage1(g, t0, w1v=w1v, bv=bv, wl1=wl1, bwl=bwl):
                    hs_i, hw = hrot.acq(g)
                    for hh in range(4):
                        def mk_h(e, t0=t0, w1v=w1v, bv=bv, hh=hh):
                            ps = pS[:, hh * 512: hh * 512 + 512]
                            for cc2 in range(CC):
                                e.matmul(ps,
                                         w1v[:, cc2 * 512 + hh * 128:
                                             cc2 * 512 + hh * 128 + 128],
                                         fm(cc2, t0 * 128, 512),
                                         start=(cc2 == 0), stop=False)
                            return e.matmul(ps,
                                            bv[:, hh * 128: hh * 128 + 128],
                                            ones_bf[:, 0:512],
                                            start=False, stop=True)
                        bw = ([bank_free[hh]]
                              if bank_free[hh] is not None else [])
                        P.step("tensor", mk_h,
                               waits=[wl1, bwl] + hw + bw, inc=TE)
                        te_h = TE.mark()
                        dst = hfm[:, hs_i * 2048 + hh * 512:
                                  hs_i * 2048 + hh * 512 + 512]
                        P.step("scalar", lambda e, d=dst, hh=hh: e.activation(
                            d, pS[:, hh * 512: hh * 512 + 512], AF.Gelu),
                            waits=[te_h], inc=SE)
                        bank_free[hh] = SE.mark()
                    se_m = SE.mark()
                    return hs_i, se_m

                def stage2(g, t0, hs_i, se_m, shared=shared, ei=ei,
                           w2v=w2v, wl2=wl2, b2v=b2v, b3v=b3v,
                           b2wl=b2wl, b3wl=b3wl):
                    for tl in range(4):
                        t = t0 + tl
                        ys_i, yw = yrot.acq(g * 4 + tl)
                        yps = pO[:, ys_i * 1024:(ys_i + 1) * 1024]
                        yww = [wl2, se_m] + yw
                        if shared:
                            yww += [b2wl, b3wl]

                        def mk_y(e, t=t, tl=tl, hs_i=hs_i, yps=yps, w2v=w2v,
                                 shared=shared, b2v=b2v, b3v=b3v):
                            ins = None
                            for dc in range(2):
                                for hh in range(4):
                                    ins = e.matmul(
                                        yps[:, dc * 512: dc * 512 + 512],
                                        hfm[:, hs_i * 2048 + hh * 512
                                            + tl * 128:
                                            hs_i * 2048 + hh * 512
                                            + tl * 128 + 128],
                                        w2v[:, hh * 1024 + dc * 512:
                                            hh * 1024 + dc * 512 + 512],
                                        start=(hh == 0),
                                        stop=(hh == 3 and not shared))
                                if shared:
                                    e.matmul(
                                        yps[:, dc * 512: dc * 512 + 512],
                                        gateT_all[:, t * 128:(t + 1) * 128],
                                        b2_sb[:, dc * 512: dc * 512 + 512],
                                        start=False, stop=False)
                                    ins = e.matmul(
                                        yps[:, dc * 512: dc * 512 + 512],
                                        ones_bf[:, 0:128],
                                        (b2v if dc == 0 else b3v),
                                        start=False, stop=True)
                            return ins
                        P.step("tensor", mk_y, waits=yww, inc=TE)
                        te_y = TE.mark()
                        ya = y_acc[:, (t - tg * 8) * C:(t - tg * 8 + 1) * C]
                        yprev = ya_marks.get(t)
                        if ei == 0:
                            P.step("vector", lambda e, d=ya, s=yps, t=t:
                                   e.tensor_scalar(
                                       d, s, gate_all[:, t * E: t * E + 1],
                                       None, ALU.mult),
                                   waits=[te_y], inc=VE)
                        elif shared:
                            P.step("vector", lambda e, d=ya, s=yps:
                                   e.scalar_tensor_tensor(d, s, 1.0, d,
                                                          ALU.mult, ALU.add),
                                   waits=[te_y] +
                                   ([yprev] if yprev else []), inc=VE)
                        else:
                            P.step("vector", lambda e, d=ya, s=yps, t=t,
                                   ei=ei: e.scalar_tensor_tensor(
                                       d, s, gate_all[:, t * E + ei:
                                                      t * E + ei + 1],
                                       d, ALU.mult, ALU.add),
                                   waits=[te_y] +
                                   ([yprev] if yprev else []), inc=VE)
                        ya_marks[t] = VE.mark()
                        yrot.release(g * 4 + tl, VE)
                    hrot.release(g, TE)

                def mkdone(wr1=wr1, wi1=wi1, wr2=wr2, wi2=wi2,
                           br1=br1, bj1=bj1, shared=shared,
                           br2=(br2 if shared else None),
                           bj2=(bj2 if shared else None),
                           br3=(br3 if shared else None),
                           bj3=(bj3 if shared else None)):
                    def done():
                        wr1.release(wi1, TE)
                        wr2.release(wi2, TE)
                        br1.release(bj1, TE)
                        if shared:
                            br2.release(bj2, TE)
                            br3.release(bj3, TE)
                    return done

                for gi, t0 in enumerate((tg * 8, tg * 8 + 4)):
                    g = git[0]
                    git[0] += 1
                    hs_i, se_m = stage1(g, t0)
                    h_last[0] = TE.mark()
                    pa.append((
                        (lambda g=g, t0=t0, hs_i=hs_i, se_m=se_m,
                         f=stage2: f(g, t0, hs_i, se_m)),
                        mkdone() if gi == 1 else None))
                    if len(pa) > 1:
                        f2a, dn = pa.pop(0)
                        f2a()
                        if dn is not None:
                            dn()
            drain_all()
            ve_ffn = VE.mark()
            for t in tiles:
                sl, w = xrot.acq(t)
                P.step("sync", lambda e, sl=sl, t=t: e.dma_start(
                    out=xstage[:, sl * C:(sl + 1) * C],
                    in_=x_src[t * 128:(t + 1) * 128, :]), waits=w,
                    inc=LDs[sl], inc_n=16)
                ld_m = LDs[sl].mark()
                osl, ow = orot.acq(t)
                xo = xostage[:, osl * C:(osl + 1) * C]
                ya = y_acc[:, (t - tg * 8) * C:(t - tg * 8 + 1) * C]
                P.step("vector", lambda e, xo=xo, ya=ya, sl=sl:
                       e.tensor_tensor(xo, xstage[:, sl * C:(sl + 1) * C],
                                       ya, ALU.add),
                       waits=[ld_m, ve_ffn] + ow, inc=VE)
                xrot.release(t, VE)
                ve_m = VE.mark()
                P.step("sync", lambda e, xo=xo, t=t: e.dma_start(
                    out=x_dst[t * 128:(t + 1) * 128, :], in_=xo),
                    waits=[ve_m], inc=STs[osl], inc_n=16)
                orot.release(t, STs[osl])

    emit_setup()
    barrier()
    npass = 1 if n_sub <= 4 else 2
    for p in range(npass):
        x0 = x_d[p]
        nhere = min(4, n_sub - p * 4)

        def dst_for(s, default):
            return y_d[p] if (s == nhere - 1 and nhere < 4) else default

        lm, bm = emit_ln(x0)
        emit_qkv(False, ln_marks=lm, bank_marks=bm, wblocks=[
            ("q", 0, wqkv_d[:, 0:512], bqkv_d[None, 0:512]),
            ("q", 8, wqkv_d[:, 512:1024], bqkv_d[None, 512:1024]),
            ("k", 0, wqkv_d[:, 1024:1536], bqkv_d[None, 1024:1536]),
            ("k", 8, wqkv_d[:, 1536:2048], bqkv_d[None, 1536:2048]),
            ("v", 0, wqkv_d[:, 2048:2560], bqkv_d[None, 2048:2560]),
            ("v", 8, wqkv_d[:, 2560:3072], bqkv_d[None, 2560:3072])])
        barrier()
        emit_attn(False, biasA)
        barrier()
        emit_proj(wproj_d, bproj_d, x0, dst_for(0, xa_d))
        barrier()
        if nhere <= 1:
            continue
        lm, bm = emit_ln(xa_d)
        emit_gate(ln_marks=lm, bank_marks=bm)
        barrier()
        emit_ffn(xa_d, dst_for(1, xb_d))
        barrier()
        if nhere <= 2:
            continue
        lm, bm = emit_ln(xb_d)
        emit_qkv(True, ln_marks=lm, bank_marks=bm, wblocks=[
            ("q", 0, wqT_d[:, 0:512], None),
            ("q", 8, wqT_d[:, 512:1024], None),
            ("k", 0, wkT_d[:, 0:512], None),
            ("k", 8, wkT_d[:, 512:1024], None),
            ("v", 0, wvT_d[:, 0:512], None),
            ("v", 8, wvT_d[:, 512:1024], None)])
        barrier()
        emit_biasB()
        barrier()
        emit_attn(True, biasB)
        barrier()
        emit_proj(woT_d, None, xb_d, dst_for(2, xa_d))
        barrier()
        if nhere <= 3:
            continue
        lm, bm = emit_ln(xa_d)
        emit_gate(ln_marks=lm, bank_marks=bm)
        barrier()
        emit_ffn(xa_d, y_d[p])
        barrier()
    P.step("sync", lambda e: e.nop(), waits=[STb.mark()])

    with nc.Block() as blk:
        P.replay(blk)
    stack.close()
    return nc


def kernel(n_sub=8, **inputs):
    g = {k: np.ascontiguousarray(np.asarray(v, dtype=np.float32))
         for k, v in inputs.items()}
    x = g["L0_attn_a_x"]

    inv = 1.0 / (10000.0 ** (np.arange(0, DH, 2, dtype=np.float64) / DH))
    t = np.arange(S, dtype=np.float64)
    fr = np.outer(t, inv)
    rcos = np.cos(fr).astype(np.float32)
    sn = np.sin(fr)
    rsin = np.concatenate([-sn, sn], axis=1).astype(np.float32)

    common = dict(
        wqkv=g["L0_attn_a_W_qkv"], bqkv=g["L0_attn_a_b_qkv"],
        wproj=g["L0_attn_a_W_proj"], bproj=g["L0_attn_a_b_proj"],
        qn=g["L0_attn_a_qn_w"], kn=g["L0_attn_a_kn_w"],
        wqT=np.ascontiguousarray(g["L0_attn_b_Wq"].T),
        wkT=np.ascontiguousarray(g["L0_attn_b_Wk"].T),
        wvT=np.ascontiguousarray(g["L0_attn_b_Wv"].T),
        woT=np.ascontiguousarray(g["L0_attn_b_Wo"].T),
        gw=g["L0_ffn_gate_W"], w1=g["L0_ffn_W1"],
        b1f=np.ascontiguousarray(g["L0_ffn_b1"].reshape(-1)),
        w2=g["L0_ffn_W2"], b2=g["L0_ffn_b2"],
        ws1=g["L0_ffn_Ws1"], bs1=g["L0_ffn_bs1"],
        ws2=g["L0_ffn_Ws2"], bs2=g["L0_ffn_bs2"],
        rs=g["L0_ffn_route_scale"].reshape(1),
        rcos=rcos, rsin=rsin,
    )
    in_maps = []
    for c in range(NCORES):
        m = dict(common)
        m["x"] = np.ascontiguousarray(x[2 * c: 2 * c + 2])
        in_maps.append(m)

    nc = build_program(n_sub)
    res = run_bass_kernel_spmd(nc, in_maps, core_ids=list(range(NCORES)))
    out = np.concatenate([np.asarray(r["y"]) for r in res.results], axis=0)
    return out.reshape(16, S, C).astype(np.float32)


# version 32 (final)
# speedup vs baseline: 1.2070x; 1.2070x over previous
import numpy as np
from contextlib import ExitStack

import concourse.bass as bass
import concourse.mybir as mybir
from concourse.bass_utils import run_bass_kernel_spmd

F32 = mybir.dt.float32
BF16 = mybir.dt.bfloat16
AF = mybir.ActivationFunctionType
ALU = mybir.AluOpType
AX = mybir.AxisListType

NCORES = 8
S = 2048
NT = 16
C = 1024
CC = 8
H = 16
DH = 64
E = 8
HD = 512
EXP_C = 30.0


class Sem:
    __slots__ = ("h", "name", "n")

    def __init__(self, h, name):
        self.h = h
        self.name = name
        self.n = 0

    def mark(self):
        return (self, self.n)


class Plan:
    ENGS = ("sync", "gpsimd", "vector", "scalar", "tensor")

    def __init__(self):
        self.steps = []

    def step(self, eng, fn, waits=(), inc=None, inc_n=1):
        w = [x for x in waits if x is not None and x[1] > 0]
        if inc is not None:
            inc.n += inc_n
        self.steps.append((eng, fn, w, (inc, inc_n) if inc is not None else None))

    def replay(self, blk):
        floors = {e: {} for e in self.ENGS}

        def run(engine_name, eng):
            fl = floors[engine_name]
            for e, fn, waits, inc in self.steps:
                if e != engine_name:
                    continue
                for (sem, val) in waits:
                    if fl.get(sem.name, 0) >= val:
                        continue
                    eng.wait_ge(sem.h, val)
                    fl[sem.name] = val
                ins = fn(eng)
                if inc is not None:
                    assert ins is not None
                    ins.then_inc(inc[0].h, inc[1])

        @blk.sync
        def _(eng):
            run("sync", eng)

        @blk.gpsimd
        def _(eng):
            run("gpsimd", eng)

        @blk.vector
        def _(eng):
            run("vector", eng)

        @blk.scalar
        def _(eng):
            run("scalar", eng)

        @blk.tensor
        def _(eng):
            run("tensor", eng)


class Rot:
    def __init__(self, n):
        self.n = n
        self.rel = [None] * n

    def acq(self, i):
        s = i % self.n
        return s, ([self.rel[s]] if self.rel[s] is not None else [])

    def release(self, i, sem):
        self.rel[i % self.n] = sem.mark()


def build_program(n_sub=8):
    nc = bass.Bass("TRN2", target_bir_lowering=False)
    P = Plan()

    def din(name, shape):
        return nc.dram_tensor(name, shape, F32, kind="ExternalInput").ap()

    x_d = din("x", [2, S, C])
    wqkv_d = din("wqkv", [C, 3 * C])
    bqkv_d = din("bqkv", [3 * C])
    wproj_d = din("wproj", [C, C])
    bproj_d = din("bproj", [C])
    qn_d = din("qn", [DH])
    kn_d = din("kn", [DH])
    wqT_d = din("wqT", [C, C])
    wkT_d = din("wkT", [C, C])
    wvT_d = din("wvT", [C, C])
    woT_d = din("woT", [C, C])
    gw_d = din("gw", [C, E])
    w1_d = din("w1", [E, C, HD])
    b1f_d = din("b1f", [E * HD])
    w2_d = din("w2", [E, HD, C])
    b2_d = din("b2", [E, C])
    ws1_d = din("ws1", [C, HD])
    bs1_d = din("bs1", [HD])
    ws2_d = din("ws2", [HD, C])
    bs2_d = din("bs2", [C])
    rs_d = din("rs", [1])
    rcos_d = din("rcos", [S, 32])
    rsin_d = din("rsin", [S, DH])

    y_d = nc.dram_tensor("y", [2, S, C], F32, kind="ExternalOutput").ap()
    xa_d = nc.dram_tensor("xsa", [S, C], F32, kind="Internal").ap()
    xb_d = nc.dram_tensor("xsb", [S, C], F32, kind="Internal").ap()

    stack = ExitStack()
    eb = stack.enter_context

    def sb(name, shape, dt=F32):
        return eb(nc.sbuf_tensor(name, shape, dt))

    big = sb("big", [128, 8 * S], BF16)
    QF = sb("QF", [128, 8 * S], BF16)
    KF = sb("KF", [128, 8 * S], BF16)
    VB = sb("VB", [128, NT * H * 65], BF16)
    wsl = sb("wsl", [128, 3 * 4096], BF16)
    xstage = sb("xstage", [128, 2 * C], F32)
    xostage = sb("xostage", [128, 2 * C], F32)
    pool = sb("pool", [128, 2 * C], F32)
    PT = sb("PT", [128, 2 * 2048], BF16)
    ghT = sb("ghT", [128, 3 * 512], BF16)
    cos_sb = sb("cos_sb", [128, NT * 32], BF16)
    sin_sb = sb("sin_sb", [128, NT * DH], BF16)
    cmaskm = sb("cmaskm", [128, 1280], BF16)
    onesq_bf = sb("onesq_bf", [128, 512], BF16)
    eye = sb("eye", [128, 128], F32)
    brow = sb("brow", [1, 4 * 512], BF16)
    b2_sb = sb("b2_sb", [8, C], BF16)
    gw_sb = sb("gw_sb", [128, CC * E], BF16)
    ones_bf = sb("ones_bf", [1, 512], BF16)
    ones_f32 = sb("ones_f32", [1, 128], F32)
    stats = sb("stats", [128, 32], F32)
    nrms = [sb(f"nrm{i}", [128, 16], F32) for i in range(4)]
    nrmq = sb("nrmq", [128, 16], F32)
    nrmk = sb("nrmk", [128, 16], F32)
    gate_all = sb("gate_all", [128, NT * E], F32)
    gwork = sb("gwork", [128, 64], F32)
    qn_c = sb("qn_c", [64, 1], F32)
    kn_c = sb("kn_c", [64, 1], F32)
    qn_r = sb("qn_r", [1, DH], F32)
    kn_r = sb("kn_r", [1, DH], F32)
    rs_sb = sb("rs_sb", [1, 1], F32)
    rs_bc = sb("rs_bc", [128, 1], F32)
    biasA = sb("biasA", [128, 1], F32)
    biasB = sb("biasB", [128, 1], F32)
    smallw = sb("smallw", [1, 16], F32)
    eps5_c = sb("eps5_c", [128, 1], F32)
    eps6_c = sb("eps6_c", [128, 1], F32)
    gneg_c = sb("gneg_c", [128, 1], F32)

    onesq = xstage[:, 0:128]
    sums_sb = pool[0:1, 0:S]
    recip_sb = xostage[0:1, 0:S]
    Rsb = xostage[0:64, 0:S]
    gateT_all = PT[0:8, 0:S]

    pS = eb(nc.psum_tensor("pS", [128, 2048], F32))
    pO = eb(nc.psum_tensor("pO", [128, 2048], F32))

    LD = Sem(eb(nc.semaphore("LD")), "LD")
    STb = Sem(eb(nc.semaphore("STb")), "STb")
    WL = Sem(eb(nc.semaphore("WL")), "WL")
    LDs = [Sem(eb(nc.semaphore(f"LDs{i}")), f"LDs{i}") for i in range(2)]
    STs = [Sem(eb(nc.semaphore(f"STs{i}")), f"STs{i}") for i in range(2)]
    WLs = [Sem(eb(nc.semaphore(f"WLs{i}")), f"WLs{i}") for i in range(3)]
    WBs = [Sem(eb(nc.semaphore(f"WBs{i}")), f"WBs{i}") for i in range(4)]
    TE = Sem(eb(nc.semaphore("TE")), "TE")
    VE = Sem(eb(nc.semaphore("VE")), "VE")
    SE = Sem(eb(nc.semaphore("SE")), "SE")
    GE = Sem(eb(nc.semaphore("GE")), "GE")

    def barrier():
        marks = ([TE.mark(), VE.mark(), SE.mark(), LD.mark(), GE.mark(),
                  WL.mark()] +
                 [s.mark() for s in LDs + STs + WLs + WBs])
        for e in ("sync", "vector", "scalar", "tensor"):
            P.step(e, lambda eng: eng.nop(), waits=marks)

    def wdma(dst_ap, src_ap, waits=()):
        P.step("gpsimd",
               lambda eng, d=dst_ap, s=src_ap: eng.dma_start(out=d, in_=s),
               waits=waits, inc=WL, inc_n=16)
        return WL.n

    wslot_rot = Rot(3)
    wslot_i = [0]

    def wslot_load(src_ap, extra=None):
        i = wslot_i[0]
        wslot_i[0] += 1
        s, w = wslot_rot.acq(i)
        dst = wsl[:, s * 4096: s * 4096 + 4096]
        a = src_ap.shape[0] // 128
        sem = WLs[s]
        sem.n += 16
        P.step("gpsimd",
               lambda eng, d=dst.rearrange("p (a n) -> p a n", a=a),
               sp=src_ap.rearrange("(a p) n -> p a n", p=128):
               eng.dma_start(out=d, in_=sp),
               waits=w + ([extra] if extra is not None else []),
               inc=None)
        P.steps[-1] = (P.steps[-1][0], P.steps[-1][1], P.steps[-1][2],
                       (sem, 16))
        return dst, (sem, sem.n), (wslot_rot, i)

    brow_rot = Rot(4)
    brow_i = [0]

    def brow_load(src_ap):
        i = brow_i[0]
        brow_i[0] += 1
        s, w = brow_rot.acq(i)
        dst = brow[:, s * 512:(s + 1) * 512]
        sem = WBs[s]
        sem.n += 16
        P.step("gpsimd", lambda eng, d=dst[:, 0:src_ap.shape[-1]],
               sp=src_ap: eng.dma_start(out=d, in_=sp), waits=w)
        P.steps[-1] = (P.steps[-1][0], P.steps[-1][1], P.steps[-1][2],
                       (sem, 16))
        return dst[:, 0:src_ap.shape[-1]], (sem, sem.n), (brow_rot, i)

    def fm(cc, t0, n):
        return big[:, cc * S + t0: cc * S + t0 + n]

    wl_tabs = [0]

    def emit_setup():
        g = P.step
        g("gpsimd", lambda e: e.memset(ones_bf[:], 1.0))
        g("gpsimd", lambda e: e.memset(ones_f32[:], 1.0))
        g("gpsimd", lambda e: e.memset(VB[:], 1.0))
        g("gpsimd", lambda e: e.memset(onesq[:], 1.0), inc=GE)
        g("gpsimd", lambda e: e.affine_select(eye[:], onesq[:], [[-1, 128]],
                                              ALU.is_equal, 0.0, base=0,
                                              channel_multiplier=1),
          waits=[GE.mark()])
        g("gpsimd", lambda e: e.memset(onesq_bf[:], 1.0), inc=GE)
        g("gpsimd", lambda e: e.memset(eps5_c[:], 1e-5))
        g("gpsimd", lambda e: e.memset(eps6_c[:], 1e-6))
        g("gpsimd", lambda e: e.memset(gneg_c[:], -1e30))
        gm = GE.mark()
        for (y0, w) in ((0, 512), (512, 512), (1024, 256)):
            g("gpsimd", lambda e, y0=y0, w=w: e.affine_select(
                cmaskm[:, y0:y0 + w], onesq_bf[:, 0:w], [[1, w]],
                ALU.is_ge, 0.0, base=y0 - 384, channel_multiplier=-1),
              waits=[gm])
        g("gpsimd", lambda e: e.nop(), inc=GE)
        ge_setup = GE.mark()
        wdma(b2_sb[:], b2_d)
        wdma(gw_sb[:].rearrange("p (a e) -> p a e", e=E),
             gw_d.rearrange("(a p) e -> p a e", p=128))
        wdma(cos_sb[:].rearrange("p (a d) -> p a d", d=32),
             rcos_d.rearrange("(a p) d -> p a d", p=128))
        wdma(sin_sb[:].rearrange("p (a d) -> p a d", d=DH),
             rsin_d.rearrange("(a p) d -> p a d", p=128))
        wl_tabs[0] = WL.n
        for dst, src in ((qn_c[:], qn_d[:, None]), (kn_c[:], kn_d[:, None]),
                         (qn_r[:], qn_d[None, :]), (kn_r[:], kn_d[None, :]),
                         (rs_sb[:], rs_d[None, :])):
            P.step("sync", lambda e, d=dst, s=src: e.dma_start(out=d, in_=s),
                   inc=LD, inc_n=16)
        ld_c = LD.n
        P.step("tensor", lambda e: e.matmul(pS[:, 0:1], ones_f32[:], rs_sb[:],
                                            start=True, stop=True),
               waits=[(LD, ld_c), ge_setup], inc=TE)
        t_rs = TE.n
        P.step("vector", lambda e: e.tensor_copy(rs_bc[:], pS[:, 0:1]),
               waits=[(TE, t_rs)], inc=VE)
        P.step("vector", lambda e: e.reduce_max(smallw[:, 0:1], qn_r[:], AX.X,
                                                apply_absolute_value=True),
               waits=[(LD, ld_c)])
        P.step("vector", lambda e: e.reduce_max(smallw[:, 1:2], kn_r[:], AX.X,
                                                apply_absolute_value=True),
               inc=VE)
        P.step("vector", lambda e: e.tensor_tensor(smallw[:, 2:3], smallw[:, 0:1],
                                                   smallw[:, 1:2], ALU.mult),
               waits=[VE.mark()], inc=VE)
        P.step("vector", lambda e: e.tensor_scalar(smallw[:, 3:4], smallw[:, 2:3],
                                                   -8.0, EXP_C, ALU.mult,
                                                   ALU.add), waits=[VE.mark()],
               inc=VE)
        v_ba = VE.n
        P.step("tensor", lambda e: e.matmul(pS[:, 4:5], ones_f32[:],
                                            smallw[:, 3:4], start=True,
                                            stop=True),
               waits=[(VE, v_ba), ge_setup], inc=TE)
        t_ba = TE.n
        P.step("vector", lambda e: e.tensor_copy(biasA[:], pS[:, 4:5]),
               waits=[(TE, t_ba)], inc=VE)

    def emit_ln(x_src):
        trash = xostage[:, 1024:2048]
        xnscr = PT[:].bitcast(F32)[:, 1024:2048]
        xrot, xnrot, trot = Rot(2), Rot(1), Rot(4)
        ln_marks = [None] * NT
        bank_marks = [None] * 4
        for u in range(NT // 2):
            st = stats[:, (u % 2) * 16:(u % 2) * 16 + 16]
            ld_ms = []
            se_sq = None
            ve_sum = None
            for tt in range(2):
                i = 2 * u + tt
                sl, w = xrot.acq(i)
                P.step("sync", lambda e, sl=sl, i=i: e.dma_start(
                    out=xstage[:, sl * C:(sl + 1) * C],
                    in_=x_src[i * 128:(i + 1) * 128, :]), waits=w,
                    inc=LDs[sl], inc_n=16)
                ld_m = LDs[sl].mark()
                ld_ms.append(ld_m)
                xs = xstage[:, sl * C:(sl + 1) * C]
                P.step("vector", lambda e, xs=xs, st=st, tt=tt: e.reduce_sum(
                    st[:, tt:tt + 1], xs, AX.X), waits=[ld_m],
                    inc=(VE if tt == 1 else None))
                P.step("scalar", lambda e, xs=xs, st=st, tt=tt: e.activation(
                    trash, xs, AF.Square, accum_out=st[:, 2 + tt:3 + tt]),
                    waits=[ld_m], inc=(SE if tt == 1 else None))
            ve_sum = VE.mark()
            se_sq = SE.mark()
            P.step("vector", lambda e, st=st: e.tensor_scalar_mul(
                st[:, 4:6], st[:, 0:2], 1.0 / C), waits=[ve_sum], inc=VE)
            P.step("vector", lambda e, st=st: e.tensor_tensor(
                st[:, 6:8], st[:, 4:6], st[:, 4:6], ALU.mult),
                waits=[VE.mark()], inc=VE)
            P.step("vector", lambda e, st=st: e.scalar_tensor_tensor(
                st[:, 8:10], st[:, 2:4], 1.0 / C, st[:, 6:8],
                ALU.mult, ALU.subtract), waits=[se_sq, VE.mark()], inc=VE)
            ve_var = VE.mark()
            P.step("scalar", lambda e, st=st: e.activation(
                st[:, 10:12], st[:, 8:10], AF.Sqrt, bias=eps5_c[:, 0:1]),
                waits=[ve_var], inc=SE)
            se_srt = SE.mark()
            P.step("vector", lambda e, st=st: e.reciprocal(
                st[:, 12:14], st[:, 10:12]), waits=[se_srt], inc=VE)
            P.step("vector", lambda e, st=st: e.tensor_tensor(
                st[:, 14:16], st[:, 4:6], st[:, 12:14], ALU.mult),
                waits=[VE.mark()], inc=VE)
            P.step("vector", lambda e, st=st: e.tensor_scalar_mul(
                st[:, 14:16], st[:, 14:16], -1.0), waits=[VE.mark()], inc=VE)
            ve_nm = VE.mark()
            for tt in range(2):
                i = 2 * u + tt
                sl = i % 2
                xs = xstage[:, sl * C:(sl + 1) * C]
                xsl, xw = xnrot.acq(i)
                xn = xnscr
                P.step("scalar", lambda e, xs=xs, st=st, xn=xn, tt=tt:
                       e.activation(xn, xs, AF.Identity,
                                    bias=st[:, 14 + tt:15 + tt],
                                    scale=st[:, 12 + tt:13 + tt]),
                       waits=[ve_nm] + xw, inc=SE)
                se_xn = SE.mark()
                xrot.release(i, SE)
                for half in range(2):
                    bk = (i * 2 + half) % 4
                    ts, tw = trot.acq(i * 2 + half)
                    for j4 in range(4):
                        j = half * 4 + j4
                        ps = pS[:, bk * 512 + j4 * 128:
                                bk * 512 + j4 * 128 + 128]
                        P.step("tensor", lambda e, xn=xn, j=j, ps=ps:
                               e.transpose(ps, xn[:, j * 128:(j + 1) * 128],
                                           eye[:]),
                               waits=([se_xn] + tw) if j4 == 0 else [],
                               inc=(TE if j4 == 3 else None))
                    te_t = TE.mark()
                    dst = bass.AP(big[:].tensor, half * 4 * S + i * 128,
                                  [[8 * S, 128], [S, 4], [1, 128]])
                    srcv = pS[:, bk * 512: bk * 512 + 512].rearrange(
                        "p (a n) -> p a n", a=4)
                    P.step("vector", lambda e, d=dst, s=srcv: e.tensor_copy(
                        d, s), waits=[te_t], inc=VE)
                    trot.release(i * 2 + half, VE)
                    bank_marks[bk] = VE.mark()
                ln_marks[i] = VE.mark()
                xnrot.release(i, TE)
        return ln_marks, bank_marks

    def emit_qkv(is_b, wblocks, ln_marks=None, bank_marks=None):
        ropescr = pool[:, 0:1024]
        qkvscr = pool[:, 1024:2048] if is_b else pool
        sqscr = xostage[:, 0:1024]
        nscr = 2 if is_b else 4
        mmrot, scrot, sqrot, trot = Rot(4), Rot(nscr), Rot(2), Rot(4)
        if bank_marks is not None:
            mmrot.rel = list(bank_marks)
        rope_prev = [None]
        it = [0]
        qpend = []
        for bi, (kind, hb, wsrc, bsrc) in enumerate(wblocks):
            wv, wl, (wr, wi) = wslot_load(wsrc)
            bv = None
            bwl = None
            if bsrc is not None:
                bv, bwl, (br, bi2) = brow_load(bsrc)

            def stage1(t, k, wv=wv, bv=bv, wl=wl, bwl=bwl):
                ms, mw = mmrot.acq(k)
                pslot = pS[:, ms * 512:(ms + 1) * 512]
                ww = [wl] + ([bwl] if bwl is not None else [])
                if ln_marks is not None:
                    ww.append(ln_marks[t])

                def mk_mm(e, pslot=pslot, wv=wv, t=t, bv=bv):
                    ins = None
                    for cc2 in range(CC):
                        ins = e.matmul(pslot, fm(cc2, t * 128, 128),
                                       wv[:, cc2 * 512:(cc2 + 1) * 512],
                                       start=(cc2 == 0),
                                       stop=(cc2 == CC - 1 and bv is None))
                    if bv is not None:
                        ins = e.matmul(pslot, ones_bf[:, 0:128], bv,
                                       start=False, stop=True)
                    return ins
                P.step("tensor", mk_mm, waits=ww + mw, inc=TE)
                return pslot, TE.mark()

            def stage2(t, k, pslot, te_mm, kind=kind, hb=hb):
                if kind == "v":
                    dst = bass.AP(VB[:].tensor, t * H * 65 + hb * 65,
                                  [[NT * H * 65, 128], [65, 8], [1, 64]])
                    P.step("vector", lambda e, d=dst, ps=pslot: e.tensor_copy(
                        d, ps.rearrange("p (h d) -> p h d", d=64)),
                        waits=[te_mm], inc=VE)
                    mmrot.release(k, VE)
                    return
                ss, sw = scrot.acq(k)
                scr = qkvscr[:, ss * 512:(ss + 1) * 512]
                nsl = k % 4
                P.step("scalar", lambda e, d=scr, ps=pslot: e.activation(
                    d, ps, AF.Copy), waits=[te_mm] + sw, inc=SE)
                mmrot.release(k, SE)
                se_cp = SE.mark()
                qs, qw = sqrot.acq(k)
                sq = sqscr[:, qs * 512:(qs + 1) * 512]
                P.step("scalar", lambda e, d=sq, s=scr: e.activation(
                    d, s, AF.Square), waits=qw + [se_cp], inc=SE)
                se_sq = SE.mark()
                n8 = nrms[nsl][:, 0:8]
                P.step("vector", lambda e, d=n8, s=sq: e.tensor_reduce(
                    d, s.rearrange("p (h d) -> p h d", d=64), AX.X, ALU.add),
                    waits=[se_sq], inc=VE)
                sqrot.release(k, VE)
                if not is_b:
                    r8 = nrms[nsl][:, 8:16]
                    ve_n8 = VE.mark()
                    P.step("scalar", lambda e, d=r8, s=n8: e.activation(
                        d, s, AF.Sqrt, scale=1.0 / DH, bias=eps6_c[:, 0:1]),
                        waits=[ve_n8], inc=SE)
                    se_r = SE.mark()
                    P.step("vector", lambda e, d=r8: e.reciprocal(d, d),
                           waits=[se_r], inc=VE)
                    ve_rc = VE.mark()
                    r8v = bass.AP(r8.tensor, r8.offset,
                                  [[16, 128], [1, 8], [0, 64]])
                    P.step("vector", lambda e, s=scr, rv=r8v:
                           e.tensor_tensor(
                               s.rearrange("p (h d) -> p h d", d=64),
                               s.rearrange("p (h d) -> p h d", d=64),
                               rv, ALU.mult),
                           waits=[ve_rc], inc=VE)
                    ve_ready = VE.mark()
                else:
                    nm_dst = (nrmq if kind == "q" else nrmk)[:, hb:hb + 8]
                    if t == 0:
                        P.step("vector", lambda e, d=nm_dst, s=n8:
                               e.tensor_copy(d, s))
                    else:
                        P.step("vector", lambda e, d=nm_dst, s=n8:
                               e.tensor_tensor(d, d, s, ALU.max))
                    cosv = bass.AP(cos_sb[:].tensor, t * 32,
                                   [[NT * 32, 128], [0, 8], [0, 2], [1, 32]])
                    sinv = bass.AP(sin_sb[:].tensor, t * DH,
                                   [[NT * DH, 128], [0, 8], [32, 2], [1, 32]])
                    swv = bass.AP(pool[:].tensor, 1024 + ss * 512 + 32,
                                  [[2048, 128], [64, 8], [-32, 2], [1, 32]])
                    tmp = ropescr[:, 0:512]
                    cq = ropescr[:, 512:1024]
                    P.step("gpsimd", lambda e, tmp=tmp, swv=swv, sinv=sinv:
                           e.tensor_tensor(
                               tmp.rearrange("p (h x d) -> p h x d", x=2, d=32),
                               swv, sinv, ALU.mult),
                           waits=[se_sq, (WL, wl_tabs[0])] +
                           ([rope_prev[0]] if rope_prev[0] else []), inc=GE)
                    P.step("gpsimd", lambda e, cq=cq, s=scr, cosv=cosv:
                           e.tensor_tensor(
                               cq.rearrange("p (h x d) -> p h x d", x=2, d=32),
                               s.rearrange("p (h x d) -> p h x d", x=2, d=32),
                               cosv, ALU.mult), waits=[GE.mark()], inc=GE)
                    P.step("vector", lambda e, s=scr, tmp=tmp, cq=cq:
                           e.tensor_tensor(s, tmp, cq, ALU.add),
                           waits=[se_sq, GE.mark()], inc=VE)
                    ve_ready = VE.mark()
                    rope_prev[0] = ve_ready
                dstbuf = QF if kind == "q" else KF
                scale_c = (qn_c if kind == "q" else kn_c) if not is_b else None
                for half in range(2):
                    bk, tw2 = trot.acq(k * 2 + half)
                    for j4 in range(4):
                        j = half * 4 + j4
                        ps = pO[:, bk * 512 + j4 * 128:
                                bk * 512 + j4 * 128 + 128]
                        P.step("tensor", lambda e, s=scr, j=j, ps=ps:
                               e.transpose(ps[0:64, :],
                                           s[:, j * 64:(j + 1) * 64], eye[:]),
                               waits=([ve_ready] + tw2) if j4 == 0 else [],
                               inc=(TE if j4 == 3 else None))
                    te_tr = TE.mark()
                    h0 = hb + half * 4
                    for par in range(2):
                        row = ((h0 + par) % 2) * 64
                        pt0 = (h0 + par) // 2
                        dst = bass.AP(dstbuf[:].tensor,
                                      row * 8 * S + pt0 * S + t * 128,
                                      [[8 * S, 64], [S, 2], [1, 128]])
                        srcv = bass.AP(pO[:].tensor,
                                       bk * 512 + par * 128,
                                       [[2048, 64], [256, 2], [1, 128]])
                        if scale_c is not None:
                            P.step("vector", lambda e, d=dst, s=srcv,
                                   sc=scale_c:
                                   e.tensor_scalar_mul(d, s, sc[:, 0:1]),
                                   waits=[te_tr], inc=VE)
                        else:
                            P.step("vector", lambda e, d=dst, s=srcv:
                                   e.tensor_copy(d, s),
                                   waits=[te_tr], inc=VE)
                    trot.release(k * 2 + half, VE)
                scrot.release(k, TE)

            def mkdone(wr=wr, wi=wi, has_b=(bsrc is not None)):
                br_, bi_ = (br, bi2) if bsrc is not None else (None, None)

                def done():
                    wr.release(wi, TE)
                    if has_b:
                        br_.release(bi_, TE)
                return done

            for t in range(NT):
                k = it[0]
                it[0] += 1
                s1 = stage1(t, k)
                qpend.append((
                    (lambda t=t, k=k, ps=s1[0], te=s1[1], f=stage2:
                     f(t, k, ps, te)),
                    mkdone() if t == NT - 1 else None))
                if len(qpend) > 2:
                    f2, dn = qpend.pop(0)
                    f2()
                    if dn is not None:
                        dn()

        while qpend:
            f2, dn = qpend.pop(0)
            f2()
            if dn is not None:
                dn()

    def emit_biasB():
        P.step("vector", lambda e: e.reduce_max(nrms[0][:, 0:1], nrmq[:], AX.X))
        P.step("vector", lambda e: e.reduce_max(nrms[0][:, 1:2], nrmk[:], AX.X),
               inc=VE)
        v0 = VE.mark()
        P.step("tensor", lambda e: e.transpose(pS[0:1, 0:128], nrms[0][:, 0:1],
                                               eye[:]), waits=[v0])
        P.step("tensor", lambda e: e.transpose(pS[0:1, 128:256], nrms[0][:, 1:2],
                                               eye[:]), inc=TE)
        t0 = TE.mark()
        P.step("vector", lambda e: e.reduce_max(smallw[:, 4:5],
                                                pS[0:1, 0:128], AX.X),
               waits=[t0])
        P.step("vector", lambda e: e.reduce_max(smallw[:, 5:6],
                                                pS[0:1, 128:256], AX.X),
               inc=VE)
        P.step("vector", lambda e: e.tensor_tensor(
            smallw[:, 6:7], smallw[:, 4:5], smallw[:, 5:6], ALU.mult),
            waits=[VE.mark()], inc=VE)
        v1 = VE.mark()
        P.step("scalar", lambda e: e.activation(smallw[:, 7:8], smallw[:, 6:7],
                                                AF.Sqrt), waits=[v1], inc=SE)
        s0 = SE.mark()
        P.step("vector", lambda e: e.tensor_scalar(
            smallw[:, 8:9], smallw[:, 7:8], -0.125, EXP_C, ALU.mult, ALU.add),
            waits=[s0], inc=VE)
        v2 = VE.mark()
        P.step("tensor", lambda e: e.matmul(pS[:, 8:9], ones_f32[:],
                                            smallw[:, 8:9], start=True,
                                            stop=True), waits=[v2], inc=TE)
        t1 = TE.mark()
        P.step("vector", lambda e: e.tensor_copy(biasB[:], pS[:, 8:9]),
               waits=[t1], inc=VE)

    attn_state = {"s_free": [None, None],
                  "pt_free": [None, None, None, None],
                  "pO_free": [None, None], "par": 0}

    def emit_attn(causal, bias_bc):
        st_ = attn_state
        tail_pending = [None]

        def emit_sweep(pair, qc):
            par = st_["par"]
            st_["par"] ^= 1
            kt_hi = min(NT - 1, qc * 4 + 3) if causal else NT - 1

            def stage1(kt):
                sl = kt % 2
                waits = ([st_["s_free"][sl]]
                         if st_["s_free"][sl] is not None else [])

                def mk_s(e, kt=kt, pair=pair, qc=qc, sl=sl):
                    ins = None
                    for side in range(2):
                        r0 = side * 64
                        ins = e.matmul(
                            pS[:, sl * 1024 + side * 512:
                               sl * 1024 + side * 512 + 512],
                            KF[r0:r0 + 64, pair * S + kt * 128:
                               pair * S + kt * 128 + 128],
                            QF[r0:r0 + 64, pair * S + qc * 512:
                               pair * S + qc * 512 + 512],
                            start=True, stop=True, tile_position=(r0, 0))
                    return ins
                P.step("tensor", mk_s, waits=waits, inc=TE)
                te_s = TE.mark()
                pt4 = kt % 4
                ptw = ([st_["pt_free"][pt4]]
                       if st_["pt_free"][pt4] is not None else [])
                P.step("scalar", lambda e, sl=sl, pt4=pt4:
                       e.activation(PT[:, pt4 * 1024: pt4 * 1024 + 1024],
                                    pS[:, sl * 1024: sl * 1024 + 1024],
                                    AF.Exp, bias=bias_bc[:, 0:1], scale=0.125),
                       waits=[te_s] + ptw, inc=SE)
                st_["s_free"][sl] = SE.mark()
                se_exp = SE.mark()
                ve_mask = None
                if causal and kt // 4 == qc:
                    moff = 384 - (kt % 4) * 128
                    pv = bass.AP(PT[:].tensor, pt4 * 1024,
                                 [[2 * 2048, 128], [512, 2], [1, 512]])
                    mv = bass.AP(cmaskm[:].tensor, moff,
                                 [[1280, 128], [0, 2], [1, 512]])
                    P.step("vector", lambda e, pv=pv, mv=mv:
                           e.tensor_tensor(pv, pv, mv, ALU.mult),
                           waits=[se_exp], inc=VE)
                    ve_mask = VE.mark()
                return se_exp, ve_mask

            def stage2(kt, se_exp, ve_mask):
                pt4 = kt % 4
                pvw = [se_exp] + ([ve_mask] if ve_mask is not None else [])
                if kt == 0 and st_["pO_free"][par] is not None:
                    pvw.append(st_["pO_free"][par])

                def mk_pv(e, kt=kt, pt4=pt4, pair=pair, par=par):
                    ins = None
                    for side in range(2):
                        h = 2 * pair + side
                        ins = e.matmul(
                            pO[0:65, par * 1024 + side * 512:
                               par * 1024 + side * 512 + 512],
                            VB[:, kt * H * 65 + h * 65:
                               kt * H * 65 + h * 65 + 65],
                            PT[:, pt4 * 1024 + side * 512:
                               pt4 * 1024 + side * 512 + 512],
                            start=(kt == 0), stop=(kt == kt_hi))
                    return ins
                P.step("tensor", mk_pv, waits=pvw, inc=TE)
                st_["pt_free"][pt4] = TE.mark()

            pend = None
            first = True
            for kt in range(kt_hi + 1):
                s1 = stage1(kt)
                if first and tail_pending[0] is not None:
                    tail_pending[0]()
                    tail_pending[0] = None
                first = False
                if pend is not None:
                    stage2(*pend)
                pend = (kt, s1[0], s1[1])
            stage2(*pend)
            te_pv = TE.mark()

            def mk_tail(pair=pair, qc=qc, par=par, te_pv=te_pv):
                rrow = xostage[0:1, par * 1024: par * 1024 + 1024]
                lnrow = pool[0:1, 0:1024]
                P.step("scalar", lambda e, par=par: e.activation(
                    lnrow, pO[64:65, par * 1024: par * 1024 + 1024], AF.Ln),
                    waits=[te_pv], inc=SE)
                P.step("scalar", lambda e, rrow=rrow: e.activation(
                    rrow, lnrow, AF.Exp, scale=-1.0),
                    waits=[SE.mark()], inc=SE)
                ve_rec = SE.mark()

                def mk_r(e, par=par, rrow=rrow):
                    ins = None
                    for side in range(2):
                        ins = e.matmul(
                            pO[64:128, par * 1024 + side * 512:
                               par * 1024 + side * 512 + 512],
                            ones_f32[:, 0:64],
                            rrow[:, side * 512: side * 512 + 512],
                            start=True, stop=True)
                    return ins
                P.step("tensor", mk_r, waits=[ve_rec], inc=TE)
                te_r = TE.mark()
                rs64 = xostage[0:64, par * 1024: par * 1024 + 1024]
                P.step("vector", lambda e, d=rs64, par=par: e.tensor_copy(
                    d, pO[64:128, par * 1024: par * 1024 + 1024]),
                    waits=[te_r], inc=VE)
                vcpy = VE.mark()
                for side in range(2):
                    dst = big[side * 64: side * 64 + 64,
                              pair * S + qc * 512: pair * S + qc * 512 + 512]
                    P.step("vector", lambda e, d=dst, side=side, par=par,
                           rs64=rs64: e.tensor_tensor(
                               d,
                               pO[0:64, par * 1024 + side * 512:
                                  par * 1024 + side * 512 + 512],
                               rs64[:, side * 512: side * 512 + 512],
                               ALU.mult),
                           waits=[vcpy], inc=VE)
                st_["pO_free"][par] = VE.mark()
            tail_pending[0] = mk_tail

        for pair in range(8):
            for qc in range(4):
                emit_sweep(pair, qc)
        tail_pending[0]()
        tail_pending[0] = None

    def emit_proj(w_dram, b_dram, x_src, x_dst):
        wvs = [wslot_load(w_dram[:, db * 512:(db + 1) * 512])
               for db in range(2)]
        bvs = ([brow_load(b_dram[None, db * 512:(db + 1) * 512])
                for db in range(2)] if b_dram is not None else None)
        xrot, orot, prot = Rot(2), Rot(2), Rot(2)
        for t in range(NT):
            sl, w = xrot.acq(t)
            P.step("sync", lambda e, sl=sl, t=t: e.dma_start(
                out=xstage[:, sl * C:(sl + 1) * C],
                in_=x_src[t * 128:(t + 1) * 128, :]), waits=w,
                inc=LDs[sl], inc_n=16)
            ld_m = LDs[sl].mark()
            ps_i, pw = prot.acq(t)
            pslot = pS[:, ps_i * 1024:(ps_i + 1) * 1024]
            for db in range(2):
                wv, wl, _ = wvs[db]
                bv, bwl = (bvs[db][0], bvs[db][1]) if bvs is not None \
                    else (None, None)
                ww = [wl] + ([bwl] if bv is not None else [])

                def mk(e, wv=wv, db=db, t=t, pslot=pslot, bv=bv):
                    ins = None
                    for cc2 in range(CC):
                        ins = e.matmul(
                            pslot[:, db * 512: db * 512 + 512],
                            big[:, cc2 * S + t * 128: cc2 * S + t * 128 + 128],
                            wv[:, cc2 * 512:(cc2 + 1) * 512],
                            start=(cc2 == 0),
                            stop=(cc2 == CC - 1 and bv is None))
                    if bv is not None:
                        ins = e.matmul(pslot[:, db * 512: db * 512 + 512],
                                       ones_bf[:, 0:128], bv,
                                       start=False, stop=True)
                    return ins
                P.step("tensor", mk, waits=ww + (pw if db == 0 else []),
                       inc=TE)
            te_m = TE.mark()
            osl, ow = orot.acq(t)
            xo = xostage[:, osl * C:(osl + 1) * C]
            P.step("vector", lambda e, xo=xo, pslot=pslot, sl=sl:
                   e.tensor_tensor(xo, xstage[:, sl * C:(sl + 1) * C], pslot,
                                   ALU.add),
                   waits=[te_m, ld_m] + ow, inc=VE)
            prot.release(t, VE)
            xrot.release(t, VE)
            ve_m = VE.mark()
            P.step("sync", lambda e, xo=xo, t=t: e.dma_start(
                out=x_dst[t * 128:(t + 1) * 128, :], in_=xo),
                waits=[ve_m], inc=STs[osl], inc_n=16)
            orot.release(t, STs[osl])
        for db in range(2):
            _, _, (wr, wi) = wvs[db]
            wr.release(wi, TE)
            if bvs is not None:
                _, _, (br2, bi2) = bvs[db]
                br2.release(bi2, TE)

    def emit_gate(ln_marks=None, bank_marks=None):
        g3 = gate_all[:].rearrange("p (t e) -> p t e", e=E)
        m1 = gwork[:, 0:16]
        m2 = gwork[:, 16:32]
        m1b = bass.AP(gwork[:].tensor, 0, [[64, 128], [1, 16], [0, E]])
        m2b = bass.AP(gwork[:].tensor, 16, [[64, 128], [1, 16], [0, E]])
        ms3 = pS[:, 512:512 + NT * E].rearrange("p (t e) -> p t e", e=E)
        for t in range(NT):
            pslot = pS[:, t * E: t * E + E]

            def mk(e, t=t, pslot=pslot):
                ins = None
                for cc2 in range(CC):
                    ins = e.matmul(pslot, fm(cc2, t * 128, 128),
                                   gw_sb[:, cc2 * E:(cc2 + 1) * E],
                                   start=(cc2 == 0), stop=(cc2 == CC - 1))
                return ins
            w = []
            if t == 0 and bank_marks is not None:
                w.append(bank_marks[0])
            if ln_marks is not None:
                w.append(ln_marks[t])
            P.step("tensor", mk, waits=w, inc=TE)
        te_g = TE.mark()
        bm1 = [bank_marks[1]] if bank_marks is not None else []
        P.step("scalar", lambda e: e.activation(
            gate_all[:], pS[:, 0:NT * E], AF.Sigmoid), waits=[te_g], inc=SE)
        P.step("vector", lambda e: e.tensor_scalar_mul(
            gate_all[:], gate_all[:], rs_bc[:, 0:1]),
            waits=[SE.mark()], inc=VE)
        P.step("vector", lambda e: e.tensor_reduce(
            m1, g3, AX.X, ALU.max), waits=[VE.mark()], inc=VE)
        P.step("vector", lambda e: e.tensor_tensor(
            ms3, g3, m1b, ALU.is_equal), waits=[VE.mark()] + bm1, inc=VE)
        P.step("vector", lambda e: e.scalar_tensor_tensor(
            ms3, ms3, gneg_c[:, 0:1], g3, ALU.mult, ALU.add),
            waits=[VE.mark()], inc=VE)
        P.step("vector", lambda e: e.tensor_reduce(
            m2, ms3, AX.X, ALU.max), waits=[VE.mark()], inc=VE)
        P.step("vector", lambda e: e.tensor_tensor(
            m1, m1, m2, ALU.add), waits=[VE.mark()], inc=VE)
        P.step("vector", lambda e: e.reciprocal(m1, m1),
               waits=[VE.mark()], inc=VE)
        P.step("vector", lambda e: e.tensor_tensor(
            ms3, g3, m2b, ALU.is_ge), waits=[VE.mark()], inc=VE)
        P.step("vector", lambda e: e.tensor_tensor(
            g3, g3, ms3, ALU.mult), waits=[VE.mark()], inc=VE)
        P.step("vector", lambda e: e.tensor_tensor(
            g3, g3, m1b, ALU.mult), waits=[VE.mark()], inc=VE)
        ve_gate = VE.mark()
        cp_marks = {}
        for g4 in range(4):
            bank = g4 % 2
            for k in range(4):
                t = 4 * g4 + k
                w = []
                if k == 0:
                    w.append(ve_gate)
                    if g4 >= 2:
                        w.append(cp_marks[g4 - 2])
                P.step("tensor", lambda e, bank=bank, k=k, t=t: e.transpose(
                    pO[0:8, bank * 512 + k * 128: bank * 512 + k * 128 + 128],
                    gate_all[:, t * E:(t + 1) * E], eye[:]),
                    waits=w, inc=(TE if k == 3 else None))
            te_gt = TE.mark()
            P.step("vector", lambda e, g4=g4, bank=bank: e.tensor_copy(
                gateT_all[:, g4 * 512:(g4 + 1) * 512],
                pO[0:8, bank * 512: bank * 512 + 512]),
                waits=[te_gt], inc=VE)
            cp_marks[g4] = VE.mark()

    def emit_ffn(x_src, x_dst):
        y_acc = QF[:].bitcast(F32)
        hfm = pool[:].bitcast(BF16)
        hrot = Rot(2)
        yrot = Rot(2)
        xrot, orot = Rot(2), Rot(2)
        ya_marks = {}
        pa = []
        pb = []
        h_last = [None]
        bank_free = [None] * 4

        def drain_all():
            while pa:
                f2a, dn = pa.pop(0)
                f2a()
                if dn is not None:
                    dn()
            while pb:
                f2b, dn = pb.pop(0)
                f2b()
                if dn is not None:
                    dn()

        git = [0]
        for tg in range(2):
            tiles = list(range(tg * 8, tg * 8 + 8))
            for ei in range(E + 1):
                shared = (ei == E)
                if shared:
                    w1v, wl1, (wr1, wi1) = wslot_load(ws1_d)
                    w2v, wl2, (wr2, wi2) = wslot_load(ws2_d,
                                                      extra=h_last[0])
                    bv, bwl, (br1, bj1) = brow_load(bs1_d[None, :])
                    b2v, b2wl, (br2, bj2) = brow_load(bs2_d[None, 0:512])
                    b3v, b3wl, (br3, bj3) = brow_load(bs2_d[None, 512:1024])
                else:
                    w1v, wl1, (wr1, wi1) = wslot_load(w1_d[ei])
                    w2v, wl2, (wr2, wi2) = wslot_load(w2_d[ei],
                                                      extra=h_last[0])
                    bv, bwl, (br1, bj1) = brow_load(
                        b1f_d[None, ei * HD: ei * HD + 512])
                    b2v = b3v = b2wl = b3wl = None

                def stage1(g, t0, w1v=w1v, bv=bv, wl1=wl1, bwl=bwl):
                    hs_i, hw = hrot.acq(g)
                    for hh in range(4):
                        def mk_h(e, t0=t0, w1v=w1v, bv=bv, hh=hh):
                            ps = pS[:, hh * 512: hh * 512 + 512]
                            for cc2 in range(CC):
                                e.matmul(ps,
                                         w1v[:, cc2 * 512 + hh * 128:
                                             cc2 * 512 + hh * 128 + 128],
                                         fm(cc2, t0 * 128, 512),
                                         start=(cc2 == 0), stop=False)
                            return e.matmul(ps,
                                            bv[:, hh * 128: hh * 128 + 128],
                                            ones_bf[:, 0:512],
                                            start=False, stop=True)
                        bw = ([bank_free[hh]]
                              if bank_free[hh] is not None else [])
                        P.step("tensor", mk_h,
                               waits=[wl1, bwl] + hw + bw, inc=TE)
                        te_h = TE.mark()
                        dst = hfm[:, hs_i * 2048 + hh * 512:
                                  hs_i * 2048 + hh * 512 + 512]
                        P.step("scalar", lambda e, d=dst, hh=hh: e.activation(
                            d, pS[:, hh * 512: hh * 512 + 512], AF.Gelu),
                            waits=[te_h], inc=SE)
                        bank_free[hh] = SE.mark()
                    se_m = SE.mark()
                    return hs_i, se_m

                def stage2(g, t0, hs_i, se_m, shared=shared, ei=ei,
                           w2v=w2v, wl2=wl2, b2v=b2v, b3v=b3v,
                           b2wl=b2wl, b3wl=b3wl):
                    for tl in range(4):
                        t = t0 + tl
                        ys_i, yw = yrot.acq(g * 4 + tl)
                        yps = pO[:, ys_i * 1024:(ys_i + 1) * 1024]
                        yww = [wl2, se_m] + yw
                        if shared:
                            yww += [b2wl, b3wl]

                        def mk_y(e, t=t, tl=tl, hs_i=hs_i, yps=yps, w2v=w2v,
                                 shared=shared, b2v=b2v, b3v=b3v):
                            ins = None
                            for dc in range(2):
                                for hh in range(4):
                                    ins = e.matmul(
                                        yps[:, dc * 512: dc * 512 + 512],
                                        hfm[:, hs_i * 2048 + hh * 512
                                            + tl * 128:
                                            hs_i * 2048 + hh * 512
                                            + tl * 128 + 128],
                                        w2v[:, hh * 1024 + dc * 512:
                                            hh * 1024 + dc * 512 + 512],
                                        start=(hh == 0),
                                        stop=(hh == 3 and not shared))
                                if shared:
                                    e.matmul(
                                        yps[:, dc * 512: dc * 512 + 512],
                                        gateT_all[:, t * 128:(t + 1) * 128],
                                        b2_sb[:, dc * 512: dc * 512 + 512],
                                        start=False, stop=False)
                                    ins = e.matmul(
                                        yps[:, dc * 512: dc * 512 + 512],
                                        ones_bf[:, 0:128],
                                        (b2v if dc == 0 else b3v),
                                        start=False, stop=True)
                            return ins
                        P.step("tensor", mk_y, waits=yww, inc=TE)
                        te_y = TE.mark()
                        ya = y_acc[:, (t - tg * 8) * C:(t - tg * 8 + 1) * C]
                        yprev = ya_marks.get(t)
                        if ei == 0:
                            P.step("vector", lambda e, d=ya, s=yps, t=t:
                                   e.tensor_scalar(
                                       d, s, gate_all[:, t * E: t * E + 1],
                                       None, ALU.mult),
                                   waits=[te_y], inc=VE)
                        elif shared:
                            P.step("vector", lambda e, d=ya, s=yps:
                                   e.scalar_tensor_tensor(d, s, 1.0, d,
                                                          ALU.mult, ALU.add),
                                   waits=[te_y] +
                                   ([yprev] if yprev else []), inc=VE)
                        else:
                            P.step("vector", lambda e, d=ya, s=yps, t=t,
                                   ei=ei: e.scalar_tensor_tensor(
                                       d, s, gate_all[:, t * E + ei:
                                                      t * E + ei + 1],
                                       d, ALU.mult, ALU.add),
                                   waits=[te_y] +
                                   ([yprev] if yprev else []), inc=VE)
                        ya_marks[t] = VE.mark()
                        yrot.release(g * 4 + tl, VE)
                    hrot.release(g, TE)

                def mkdone(wr1=wr1, wi1=wi1, wr2=wr2, wi2=wi2,
                           br1=br1, bj1=bj1, shared=shared,
                           br2=(br2 if shared else None),
                           bj2=(bj2 if shared else None),
                           br3=(br3 if shared else None),
                           bj3=(bj3 if shared else None)):
                    def done():
                        wr1.release(wi1, TE)
                        wr2.release(wi2, TE)
                        br1.release(bj1, TE)
                        if shared:
                            br2.release(bj2, TE)
                            br3.release(bj3, TE)
                    return done

                for gi, t0 in enumerate((tg * 8, tg * 8 + 4)):
                    g = git[0]
                    git[0] += 1
                    hs_i, se_m = stage1(g, t0)
                    h_last[0] = TE.mark()
                    pa.append((
                        (lambda g=g, t0=t0, hs_i=hs_i, se_m=se_m,
                         f=stage2: f(g, t0, hs_i, se_m)),
                        mkdone() if gi == 1 else None))
                    if len(pa) > 1:
                        f2a, dn = pa.pop(0)
                        f2a()
                        if dn is not None:
                            dn()
            drain_all()
            ve_ffn = VE.mark()
            for t in tiles:
                sl, w = xrot.acq(t)
                P.step("sync", lambda e, sl=sl, t=t: e.dma_start(
                    out=xstage[:, sl * C:(sl + 1) * C],
                    in_=x_src[t * 128:(t + 1) * 128, :]), waits=w,
                    inc=LDs[sl], inc_n=16)
                ld_m = LDs[sl].mark()
                osl, ow = orot.acq(t)
                xo = xostage[:, osl * C:(osl + 1) * C]
                ya = y_acc[:, (t - tg * 8) * C:(t - tg * 8 + 1) * C]
                P.step("vector", lambda e, xo=xo, ya=ya, sl=sl:
                       e.tensor_tensor(xo, xstage[:, sl * C:(sl + 1) * C],
                                       ya, ALU.add),
                       waits=[ld_m, ve_ffn] + ow, inc=VE)
                xrot.release(t, VE)
                ve_m = VE.mark()
                P.step("sync", lambda e, xo=xo, t=t: e.dma_start(
                    out=x_dst[t * 128:(t + 1) * 128, :], in_=xo),
                    waits=[ve_m], inc=STs[osl], inc_n=16)
                orot.release(t, STs[osl])

    emit_setup()
    barrier()
    npass = 1 if n_sub <= 4 else 2
    for p in range(npass):
        x0 = x_d[p]
        nhere = min(4, n_sub - p * 4)

        def dst_for(s, default):
            return y_d[p] if (s == nhere - 1 and nhere < 4) else default

        lm, bm = emit_ln(x0)
        emit_qkv(False, ln_marks=lm, bank_marks=bm, wblocks=[
            ("q", 0, wqkv_d[:, 0:512], bqkv_d[None, 0:512]),
            ("q", 8, wqkv_d[:, 512:1024], bqkv_d[None, 512:1024]),
            ("k", 0, wqkv_d[:, 1024:1536], bqkv_d[None, 1024:1536]),
            ("k", 8, wqkv_d[:, 1536:2048], bqkv_d[None, 1536:2048]),
            ("v", 0, wqkv_d[:, 2048:2560], bqkv_d[None, 2048:2560]),
            ("v", 8, wqkv_d[:, 2560:3072], bqkv_d[None, 2560:3072])])
        barrier()
        emit_attn(False, biasA)
        barrier()
        emit_proj(wproj_d, bproj_d, x0, dst_for(0, xa_d))
        barrier()
        if nhere <= 1:
            continue
        lm, bm = emit_ln(xa_d)
        emit_gate(ln_marks=lm, bank_marks=bm)
        barrier()
        emit_ffn(xa_d, dst_for(1, xb_d))
        barrier()
        if nhere <= 2:
            continue
        lm, bm = emit_ln(xb_d)
        emit_qkv(True, ln_marks=lm, bank_marks=bm, wblocks=[
            ("q", 0, wqT_d[:, 0:512], None),
            ("q", 8, wqT_d[:, 512:1024], None),
            ("k", 0, wkT_d[:, 0:512], None),
            ("k", 8, wkT_d[:, 512:1024], None),
            ("v", 0, wvT_d[:, 0:512], None),
            ("v", 8, wvT_d[:, 512:1024], None)])
        barrier()
        emit_biasB()
        barrier()
        emit_attn(True, biasB)
        barrier()
        emit_proj(woT_d, None, xb_d, dst_for(2, xa_d))
        barrier()
        if nhere <= 3:
            continue
        lm, bm = emit_ln(xa_d)
        emit_gate(ln_marks=lm, bank_marks=bm)
        barrier()
        emit_ffn(xa_d, y_d[p])
        barrier()
    P.step("sync", lambda e: e.nop(), waits=[STb.mark()])

    with nc.Block() as blk:
        P.replay(blk)
    stack.close()
    return nc


def kernel(n_sub=8, **inputs):
    g = {k: np.ascontiguousarray(np.asarray(v, dtype=np.float32))
         for k, v in inputs.items()}
    x = g["L0_attn_a_x"]

    inv = 1.0 / (10000.0 ** (np.arange(0, DH, 2, dtype=np.float64) / DH))
    t = np.arange(S, dtype=np.float64)
    fr = np.outer(t, inv)
    rcos = np.cos(fr).astype(np.float32)
    sn = np.sin(fr)
    rsin = np.concatenate([-sn, sn], axis=1).astype(np.float32)

    common = dict(
        wqkv=g["L0_attn_a_W_qkv"], bqkv=g["L0_attn_a_b_qkv"],
        wproj=g["L0_attn_a_W_proj"], bproj=g["L0_attn_a_b_proj"],
        qn=g["L0_attn_a_qn_w"], kn=g["L0_attn_a_kn_w"],
        wqT=np.ascontiguousarray(g["L0_attn_b_Wq"].T),
        wkT=np.ascontiguousarray(g["L0_attn_b_Wk"].T),
        wvT=np.ascontiguousarray(g["L0_attn_b_Wv"].T),
        woT=np.ascontiguousarray(g["L0_attn_b_Wo"].T),
        gw=g["L0_ffn_gate_W"], w1=g["L0_ffn_W1"],
        b1f=np.ascontiguousarray(g["L0_ffn_b1"].reshape(-1)),
        w2=g["L0_ffn_W2"], b2=g["L0_ffn_b2"],
        ws1=g["L0_ffn_Ws1"], bs1=g["L0_ffn_bs1"],
        ws2=g["L0_ffn_Ws2"], bs2=g["L0_ffn_bs2"],
        rs=g["L0_ffn_route_scale"].reshape(1),
        rcos=rcos, rsin=rsin,
    )
    in_maps = []
    for c in range(NCORES):
        m = dict(common)
        m["x"] = np.ascontiguousarray(x[2 * c: 2 * c + 2])
        in_maps.append(m)

    nc = build_program(n_sub)
    res = run_bass_kernel_spmd(nc, in_maps, core_ids=list(range(NCORES)))
    out = np.concatenate([np.asarray(r["y"]) for r in res.results], axis=0)
    return out.reshape(16, S, C).astype(np.float32)
